# version 13
# speedup vs baseline: 1.0282x; 1.0010x over previous
import sys
if "/opt/trn_rl_repo" not in sys.path:
    sys.path.insert(0, "/opt/trn_rl_repo")
from contextlib import ExitStack

import numpy as np
import ml_dtypes
import concourse.bass as bass
import concourse.mybir as mybir
from concourse.bass_utils import run_bass_kernel_spmd

F32 = mybir.dt.float32
BF16 = mybir.dt.bfloat16
U8 = mybir.dt.uint8
AF = mybir.ActivationFunctionType
ALU = mybir.AluOpType
AX = mybir.AxisListType

D = 1024
T = 2048
NB = 2
NTILE = T // 128
KD = D // 128
N_CORES = 8
EPS = 1e-5

SEM_NAMES = ["ld", "wst", "st", "tc", "vc", "sc", "gc", "syq"]


class Prog:
    def __init__(self):
        self.nc = bass.Bass("TRN2", target_bir_lowering=False, debug=False)
        self.stack = ExitStack()
        self.sems = {}
        self.tot = {n: 0 for n in SEM_NAMES}
        self.block = None
        self._uid = 0

    def uid(self):
        self._uid += 1
        return self._uid

    def open(self):
        for n in SEM_NAMES:
            self.sems[n] = self.stack.enter_context(self.nc.semaphore(n))
        self.block_cm = self.nc.Block()
        self.block = self.block_cm.__enter__()

    def close(self):
        self.block_cm.__exit__(None, None, None)
        self.stack.close()

    def dma(self, q, out, in_, sem="ld", **kw):
        q.dma_start(out=out, in_=in_, **kw).then_inc(self.sems[sem], 16)
        self.tot[sem] += 16

    def op(self, inst, sem):
        inst.then_inc(self.sems[sem], 1)
        self.tot[sem] += 1

    def snap(self):
        return dict(self.tot)

    def wait(self, eng, snapshot=None, only=None):
        s = snapshot if snapshot is not None else self.tot
        for n in (only if only else SEM_NAMES):
            if s[n] > 0:
                eng.wait_ge(self.sems[n], s[n])

    def wait_prep(self, eng, prep, names):
        gcm = 0
        sqm = 0
        for n in names:
            g1, s1 = prep.ms.get(n, (0, 0))
            gcm = max(gcm, g1)
            sqm = max(sqm, s1)
        if gcm:
            eng.wait_ge(self.sems["gc"], gcm)
        if sqm:
            eng.wait_ge(self.sems["syq"], sqm)

    def vfence(self, v):
        v.wait_ge(self.sems["vc"], self.tot["vc"])

    def sb(self, name, shape, dt):
        return self.stack.enter_context(
            self.nc.sbuf_tensor(f"{name}_{self.uid()}", shape, dt))

    def ps(self, name, shape, dt=F32):
        return self.stack.enter_context(
            self.nc.psum_tensor(f"{name}_{self.uid()}", shape, dt))



def host_consts():
    c = {}
    c["idn_f32"] = np.eye(128, dtype=np.float32)
    c["ones_row"] = np.ones((1, 128), dtype=np.float32)
    c["tri_lt"] = (np.arange(128)[:, None] < np.arange(128)[None, :]).astype(np.float32)
    ii = np.arange(128)[:, None]
    jj = np.arange(256)[None, :]
    band = ((jj >= ii) & (jj <= ii + 128)).astype(np.float32)
    c["band_main"] = band
    c["band_t0"] = band * (jj >= 128)
    o127 = np.ones((1, 128), dtype=np.float32)
    o127[0, 127] = 0.0
    c["ones127_row"] = o127
    c["ones_col"] = np.ones((128, 1), dtype=np.float32)
    return c


def declare_consts(p: Prog):
    return {k: p.nc.dram_tensor(k, list(v.shape), F32, kind="ExternalInput").ap()
            for k, v in host_consts().items()}


class Consts:
    def __init__(self, p: Prog, daps):
        self.idn_f = p.sb("idn_f", [128, 128], F32)
        self.idn_b = p.sb("idn_b", [128, 128], BF16)
        self.ones_f = p.sb("ones_f", [1, 128], F32)
        self.ones_b = p.sb("ones_b", [1, 128], BF16)
        self.tri = p.sb("tri", [128, 128], BF16)
        self.band_main = p.sb("band_main", [128, 256], F32)
        self.band_t0 = p.sb("band_t0", [128, 256], F32)
        self.cval = p.sb("cval", [128, 16], F32)
        self.tri_f = p.sb("tri_f", [128, 128], F32)
        self.ones32 = p.sb("ones32", [1, 128], F32)
        self.ones127_b = p.sb("ones127_b", [1, 128], BF16)
        self.ones_col = p.sb("ones_col", [128, 1], F32)
        self.daps = daps
        self.p = p

    def load(self):
        p = self.p

        @p.block.gpsimd
        def _(g):
            d = self.daps
            p.dma(g, self.idn_f[:], d["idn_f32"])
            p.dma(g, self.idn_b[:], d["idn_f32"])
            p.dma(g, self.ones_f[:], d["ones_row"])
            p.dma(g, self.ones_b[:], d["ones_row"])
            p.dma(g, self.tri[:], d["tri_lt"])
            p.dma(g, self.tri_f[:], d["tri_lt"])
            p.dma(g, self.band_main[:], d["band_main"])
            p.dma(g, self.band_t0[:], d["band_t0"])
            p.dma(g, self.ones127_b[:], d["ones127_row"])
            p.dma(g, self.ones_col[:], d["ones_col"])

        @p.block.vector
        def _(v):
            p.wait(v, only=["ld"])
            p.op(v.memset(self.cval[:, 0:1], EPS), "vc")
            p.op(v.memset(self.cval[:, 1:2], -1.0), "vc")
            p.op(v.memset(self.cval[:, 2:3], 1.0), "vc")
            p.op(v.memset(self.cval[:, 3:4], 1e-12), "vc")
            p.op(v.memset(self.cval[:, 4:5], 1e-6), "vc")
            p.op(v.memset(self.cval[:, 5:6], float(np.log(1152.0))), "vc")
            p.op(v.memset(self.cval[:, 6:7], 1.0 / D), "vc")
            p.op(v.memset(self.cval[:, 7:8], -0.5), "vc")
            p.op(v.memset(self.cval[:, 8:9], 1024.0), "vc")
            p.op(v.memset(self.cval[:, 9:10], float(np.log(32.0))), "vc")
            p.op(v.memset(self.cval[:, 10:11], 0.0), "vc")
            p.op(v.memset(self.cval[:, 11:12], 4e-4), "vc")
            p.op(v.memset(self.cval[:, 12:13], -0.02), "vc")
            p.op(v.memset(self.cval[:, 13:14], 1.0 / 4096.0), "vc")
            p.op(v.memset(self.cval[:, 14:15], 1e-30), "vc")
            p.op(v.memset(self.cval[:, 15:16], 0.5), "vc")
            p.op(v.memset(self.ones32[:], 32.0), "vc")


def load_cols(p: Prog, cst: Consts, dram_1d, n, dst_cols):
    nc = p.nc
    with nc.sbuf_tensor(f"clb_{p.uid()}", [n, 128], F32) as bsb, \
         nc.psum_tensor(f"clp_{p.uid()}", [128, max(n, 2)], F32) as bps:

        @p.block.gpsimd
        def _(g):
            p.wait(g, only=["vc", "tc"])
            p.dma(g, bsb[:], dram_1d.rearrange("(a b) -> a b", b=128), sem="ld")

        @p.block.tensor
        def _(t):
            p.wait(t, only=["ld", "vc"])
            p.op(t.transpose(bps[:, 0:n], bsb[:], cst.idn_f[0:n, 0:n]), "tc")

        @p.block.vector
        def _(v):
            p.wait(v, only=["tc"])
            p.op(v.tensor_copy(dst_cols, bps[:, 0:n]), "vc")



class Prep:
    CH = 1024

    def __init__(self, p: Prog, win):
        self.p = p
        self.win = win
        self.scr = {}
        self.bf_tmp = {}
        self.ms = {}

    def scratch_direct(self, name):
        if name not in self.scr:
            w = self.win[name]
            R, C = w.shape
            s = self.p.nc.dram_tensor(f"scr_{name}", [R, C], BF16, kind="Internal").ap()
            self.scr[name] = ("direct", s, (R, C))
        return self.scr[name][1]

    def scratch_transposed(self, name):
        if name not in self.scr:
            w = self.win[name]
            R, C = w.shape
            assert C % 128 == 0
            s = self.p.nc.dram_tensor(f"scr_{name}", [C // 128, 128, R], BF16,
                                      kind="Internal").ap()
            self.scr[name] = ("transposed", s, (R, C))
        return self.scr[name][1]

    def emit(self):
        p = self.p
        CH = self.CH
        tbuf = p.sb("prep_t", [128, 2 * CH], BF16)

        names = list(self.scr.keys())
        gc_ms = {}
        for name in names:
            kind, s, (R, C) = self.scr[name]
            if kind == "transposed":
                b = p.nc.dram_tensor(f"bft_{name}", [R, C], BF16, kind="Internal").ap()
                self.bf_tmp[name] = b

        @p.block.gpsimd
        def _(g):
            p.wait(g, only=["vc"])
            for name in names:
                kind, s, (R, C) = self.scr[name]
                dst = s if kind == "direct" else self.bf_tmp[name]
                p.dma(g, dst, self.win[name], sem="gc")
                gc_ms[name] = p.tot["gc"]

        jobs = []
        for name in names:
            kind, s, (R, C) = self.scr[name]
            if kind != "transposed":
                self.ms[name] = (gc_ms[name], 0)
                continue
            nj0 = len(jobs)
            for k in range(C // 128):
                for c0 in range(0, R, CH):
                    jobs.append([name, k, c0, min(CH, R - c0)])

        wst0 = p.tot["wst"]
        syq0 = p.tot["syq"]

        @p.block.sync
        def _(sy):
            for j, (name, k, c0, wdt) in enumerate(jobs):
                sy.wait_ge(p.sems["gc"], gc_ms[name])
                if j >= 2:
                    sy.wait_ge(p.sems["syq"], syq0 + 16 * (j - 1))
                half = (j % 2) * CH
                p.dma(sy, tbuf[:, half:half + wdt],
                      self.bf_tmp[name][c0:c0 + wdt, 128 * k:128 * (k + 1)],
                      sem="wst", transpose=True)

        @p.block.scalar
        def _(sq):
            for j, (name, k, c0, wdt) in enumerate(jobs):
                sq.wait_ge(p.sems["wst"], wst0 + 16 * (j + 1))
                half = (j % 2) * CH
                s = self.scr[name][1]
                p.dma(sq, s[k][:, c0:c0 + wdt], tbuf[:, half:half + wdt], sem="syq")
                self.ms[name] = (gc_ms[name], p.tot["syq"])



def emit_ln_tile(p, blk, cst, x_ap, cols, tmp, outs):

    @blk.vector
    def _(v):
        p.wait(v, only=["ld", "sc", "tc", "vc"])
        p.op(v.scalar_tensor_tensor(tmp, x_ap, 1.0, x_ap, ALU.mult, ALU.mult,
                                    accum_out=cols[:, 9:10]), "vc")
        p.op(v.reduce_sum(cols[:, 8:9], x_ap, AX.X), "vc")

    @blk.vector
    def _(v):
        p.vfence(v)
        p.op(v.tensor_scalar(cols[:, 10:11], cols[:, 8:9], 1.0 / D, None, ALU.mult), "vc")
        p.op(v.tensor_scalar(cols[:, 12:13], cols[:, 9:10], 1.0 / D, None, ALU.mult), "vc")

    @blk.vector
    def _(v):
        p.vfence(v)
        p.op(v.tensor_scalar(cols[:, 1:2], cols[:, 10:11], -1.0, None, ALU.mult), "vc")
        p.op(v.tensor_mul(cols[:, 11:12], cols[:, 10:11], cols[:, 10:11]), "vc")

    @blk.vector
    def _(v):
        p.vfence(v)
        p.op(v.tensor_sub(cols[:, 13:14], cols[:, 12:13], cols[:, 11:12]), "vc")

    @blk.scalar
    def _(s):
        p.wait(s, only=["vc"])
        p.op(s.activation(cols[:, 14:15], cols[:, 13:14], AF.Ln,
                          bias=cst.cval[:, 0:1]), "sc")
        p.op(s.activation(cols[:, 0:1], cols[:, 14:15], AF.Exp,
                          scale=cst.cval[:, 7:8]), "sc")

    @blk.vector
    def _(v):
        p.wait(v, only=["sc", "st"])
        for ap in outs:
            p.op(v.tensor_scalar(ap, x_ap, cols[:, 1:2], cols[:, 0:1],
                                 ALU.add, ALU.mult), "vc")


def emit_transpose_tiles(p, blk, cst, src_tm, dst_fm, tile_idx, L, pT2, nk=KD,
                         f32=False):
    tc0 = p.tot["tc"]
    vc0 = p.tot["vc"]
    idn = (cst.idn_f if f32 else cst.idn_b)[:]

    @blk.tensor
    def _(t):
        p.wait(t, only=["vc", "sc", "ld"])
        for k in range(nk):
            if k >= 2:
                t.wait_ge(p.sems["vc"], vc0 + (k - 1))
            p.op(t.transpose(pT2[k % 2][:, 0:128],
                             src_tm[:, 128 * k:128 * (k + 1)], idn), "tc")

    @blk.vector
    def _(v):
        for k in range(nk):
            v.wait_ge(p.sems["tc"], tc0 + k + 1)
            p.op(v.tensor_copy(dst_fm[:, k * L + tile_idx * 128: k * L + (tile_idx + 1) * 128],
                               pT2[k % 2][:, 0:128]), "vc")


def emit_cols_to_row(p, blk, cst, src_cols, n, dst_row, pTa, pTb):
    tc0 = p.tot["tc"]
    vc0 = p.tot["vc"]

    @blk.tensor
    def _(t):
        p.wait(t, only=["vc", "sc"])
        for j in range(n):
            if j >= 2:
                t.wait_ge(p.sems["vc"], vc0 + (j - 1))
            pt = pTa if j % 2 == 0 else pTb
            p.op(t.transpose(pt[0:1, 0:128], src_cols[:, j:j + 1], cst.idn_f[:]), "tc")

    @blk.vector
    def _(v):
        for j in range(n):
            v.wait_ge(p.sems["tc"], tc0 + j + 1)
            pt = pTa if j % 2 == 0 else pTb
            p.op(v.tensor_copy(dst_row[0:1, 128 * j:128 * (j + 1)], pt[0:1, 0:128]), "vc")


def emit_row_to_rep(p, blk, cst, src_row, L, dst_rep, pbank, lhs_row=None):
    lhs = (lhs_row if lhs_row is not None else cst.ones_f)[:]

    for c0 in range(0, L, 512):
        w = min(512, L - c0)

        @blk.tensor
        def _(t, c0=c0, w=w):
            p.wait(t, only=["sc", "vc", "ld", "gc"])
            p.op(t.matmul(pbank[:, 0:w], lhs, src_row[0:1, c0:c0 + w],
                          start=True, stop=True), "tc")

        @blk.vector
        def _(v, c0=c0, w=w):
            p.wait(v, only=["tc"])
            p.op(v.tensor_copy(dst_rep[:, c0:c0 + w], pbank[:, 0:w]), "vc")



S2_DBG = []


def emit_s2(p: Prog, cst: Consts, prep: Prep, W, xin_b, xout_b):
    nc, blk = p.nc, p.block
    LC = 512
    NCH = T // LC

    lw1s = prep.scr["L0_ffn_light_w1"][1]
    lw2s = prep.scr["L0_ffn_light_w2"][1]
    hw1s = prep.scr["L0_ffn_heavy_w1"][1]
    hw2s = prep.scr["L0_ffn_heavy_w2"][1]

    with ExitStack() as st:
        sb = lambda n, sh, dt: st.enter_context(nc.sbuf_tensor(f"{n}_{p.uid()}", sh, dt))
        ps = lambda n, sh, dt=F32: st.enter_context(nc.psum_tensor(f"{n}_{p.uid()}", sh, dt))

        yT = sb("yT", [128, KD * T], BF16)
        yThc = sb("yThc", [128, KD * 512], BF16)
        SREP = sb("SREP", [128, T], F32)
        lw1sb = sb("lw1sb", [128, 4096], BF16)
        lw2sb = sb("lw2sb", [128, 4096], BF16)
        hbuf = sb("hbuf", [128, 32 * LC], BF16)
        wslab = sb("wslab", [128, 16384], BF16)
        wsl1 = [wslab[:, 0:4096], wslab[:, 4096:8192]]
        VREP = hbuf[:, 0:4096].bitcast(F32)
        trep = wslab[:, 0:2048].bitcast(F32)
        lightb = sb("lightb", [128, 4 * D], F32)
        dump = lightb
        xtl2 = sb("xtl2", [128, 2 * D], F32)
        xtl = sb("xtl", [128, D], F32)
        cols = sb("cols", [128, 16], F32)
        tmp = sb("tmp", [128, D], F32)
        ytb = sb("ytb", [128, D], BF16)
        yf = sb("yf", [128, D], F32)
        s_tm = sb("s_tm", [128, 16], F32)
        r_tm = sb("r_tm", [128, 16], F32)
        gl_c = sb("gl_c", [128, 8], F32)
        gh_c = sb("gh_c", [128, 8], F32)
        b1l_c = sb("b1l_c", [128, 4], F32)
        b1h_c = sb("b1h_c", [128, 32], F32)
        b2l_r = sb("b2l_r", [1, D], BF16)
        b2h_r = sb("b2h_r", [1, D], BF16)
        vrow = sb("vrow", [1, T], F32)
        t_row = vrow[0:1, 0:D]
        rwork = sb("rwork", [128, 64], F32)
        r11 = sb("r11", [1, 8], F32)
        prefs = sb("prefs", [128, 16], F32)
        cg = sb("cg", [128, 16], F32)
        satu = sb("satu", [128, 16], U8)
        sel32 = sb("sel32", [128, 16], F32)
        u1 = sb("u1", [128, 2 * LC], F32)

        load_cols(p, cst, W["L0_ffn_light_gamma"], 8, gl_c[:, 0:8])
        load_cols(p, cst, W["L0_ffn_heavy_gamma"], 8, gh_c[:, 0:8])
        load_cols(p, cst, W["L0_ffn_light_b1"], 4, b1l_c[:, 0:4])
        load_cols(p, cst, W["L0_ffn_heavy_b1"], 32, b1h_c[:, 0:32])

        pA = [ps("pA0", [128, 512]), ps("pA1", [128, 512])]
        pB = [ps("pB0", [128, 512]), ps("pB1", [128, 512])]
        pC = pA[0]
        pTa = ps("pTa", [128, 128])
        pTb = ps("pTb", [128, 128])
        pTc0 = ps("pTc0", [128, 128], BF16)
        pTc1 = ps("pTc1", [128, 128], BF16)

        @blk.gpsimd
        def _(g):
            p.wait(g, only=["vc", "st"])
            p.wait_prep(g, prep, ["L0_ffn_light_w1", "L0_ffn_light_w2"])
            p.dma(g, b2l_r[:], W["L0_ffn_light_b2"].rearrange("(a d) -> a d", a=1))
            p.dma(g, b2h_r[:], W["L0_ffn_heavy_b2"].rearrange("(a d) -> a d", a=1))
            p.dma(g, t_row, W["L0_ffn_routing_token"].rearrange("(a d) -> a d", a=1))
            p.dma(g, lw1sb[:], lw1s.rearrange("(k p) o -> p k o", p=128), sem="wst")
            p.dma(g, lw2sb[:], lw2s.rearrange("(k p) o -> p k o", p=128), sem="wst")

        emit_row_to_rep(p, blk, cst, t_row, D, trep, pA[0])

        @blk.vector
        def _(v):
            p.wait(v, only=["wst", "vc", "tc"])
            for k in range(KD):
                p.op(v.tensor_scalar(lw1sb[:, k * 512:(k + 1) * 512],
                                     lw1sb[:, k * 512:(k + 1) * 512],
                                     gl_c[:, k:k + 1], None, ALU.mult), "vc")

        for c in range(4):
            for i in range(4):
                t0 = 4 * c + i

                @blk.scalar
                def _(sq, t0=t0):
                    p.wait(sq, only=["vc", "st", "sc"])
                    p.dma(sq, xtl[:], xin_b[128 * t0:128 * (t0 + 1), :], sem="ld")

                x_ap = xtl[:]
                emit_ln_tile(p, blk, cst, x_ap, cols, tmp[:], [ytb[:], yf[:]])
                emit_transpose_tiles(p, blk, cst, ytb[:], yT, t0, T, (pTc0, pTc1))

                @blk.vector
                def _(v):
                    p.wait(v, only=["sc", "vc"])
                    p.op(v.scalar_tensor_tensor(cols[:, 11:12], cols[:, 0:1], 1024.0,
                                                cols[:, 0:1], ALU.mult, ALU.mult), "vc")

                @blk.vector
                def _(v):
                    p.vfence(v)
                    p.op(v.tensor_mul(cols[:, 12:13], cols[:, 11:12], cols[:, 13:14]), "vc")

                @blk.scalar
                def _(s, t0=t0):
                    p.wait(s, only=["vc"])
                    p.op(s.activation(cols[:, 11:12], cols[:, 12:13], AF.Ln,
                                      bias=cst.cval[:, 14:15]), "sc")
                    p.op(s.activation(r_tm[:, t0:t0 + 1], cols[:, 11:12], AF.Exp,
                                      scale=cst.cval[:, 7:8], bias=cst.cval[:, 9:10]), "sc")

                @blk.vector
                def _(v, t0=t0):
                    p.vfence(v)
                    p.op(v.scalar_tensor_tensor(dump[:, 0:D], yf[:], 1.0, trep,
                                                ALU.mult, ALU.mult,
                                                accum_out=s_tm[:, t0:t0 + 1]), "vc")

        emit_cols_to_row(p, blk, cst, r_tm[:], 16, vrow, pTa, pTb)
        emit_row_to_rep(p, blk, cst, vrow, T, SREP, pA[0])

        bcol = rwork[:, 0:16]
        sbv = rwork[:, 16:32]
        rm = rwork[:, 32:33]
        nmc = rwork[:, 33:34]
        rsum = rwork[:, 34:35]
        acol = rwork[:, 35:36]
        tcol = rwork[:, 36:52]
        vv = sb("vv", [128, 16], F32)

        emit_cols_to_row(p, blk, cst, s_tm[:], 16, vrow, pTa, pTb)

        drow = dump[0:1, 0:T]
        naR = r11[0:1, 0:1]
        mxR = r11[0:1, 1:2]
        nmR = r11[0:1, 2:3]
        ssR = r11[0:1, 3:4]
        lnR = r11[0:1, 4:5]
        a11 = r11[0:1, 5:6]

        for it in range(50):
            @blk.vector
            def _(v, it=it):
                if it == 0:
                    p.wait(v, only=["vc", "sc"])
                    p.op(v.tensor_scalar(drow, vrow[0:1, 0:T], -1.0, None,
                                         ALU.mult), "vc")
                    p.vfence(v)
                    p.op(v.tensor_tensor(drow, vrow[0:1, 0:T], drow, ALU.add), "vc")
                else:
                    p.vfence(v)
                    p.op(v.tensor_scalar(drow, vrow[0:1, 0:T], naR, None,
                                         ALU.min), "vc")
                p.vfence(v)
                p.op(v.reduce_max(mxR, drow, AX.X), "vc")
                p.vfence(v)
                p.op(v.tensor_scalar(nmR, mxR, -1.0, None, ALU.mult), "vc")

            @blk.scalar
            def _(s):
                p.wait(s, only=["vc"])
                p.op(s.activation(drow, drow, AF.Exp, bias=nmR,
                                  accum_out=ssR), "sc")
                p.op(s.activation(lnR, ssR, AF.Ln), "sc")

            @blk.vector
            def _(v):
                p.wait(v, only=["sc"])
                p.op(v.tensor_tensor(naR, lnR, nmR, ALU.subtract), "vc")
                p.vfence(v)
                p.op(v.tensor_scalar(naR, naR, cst.cval[0:1, 5:6], None,
                                     ALU.subtract), "vc")

        @blk.vector
        def _(v):
            p.vfence(v)
            p.op(v.tensor_scalar(a11, naR, -1.0, None, ALU.mult), "vc")

        @blk.tensor
        def _(t):
            p.wait(t, only=["vc"])
            p.op(t.matmul(pTb[:, 0:1], cst.ones_f[:], a11,
                          start=True, stop=True), "tc")

        @blk.vector
        def _(v):
            p.wait(v, only=["tc"])
            p.op(v.tensor_copy(acol, pTb[:, 0:1]), "vc")

        @blk.vector
        def _(v):
            p.vfence(v)
            p.op(v.tensor_scalar(vv[:], s_tm[:], acol, cst.cval[:, 10:11],
                                 ALU.add, ALU.min), "vc")

        @blk.vector
        def _(v):
            p.vfence(v)
            p.op(v.tensor_scalar(prefs[:], vv[:], cst.cval[:, 10:11], None,
                                 ALU.is_ge), "vc")
            p.op(v.tensor_scalar(satu[:], vv[:], cst.cval[:, 10:11], None,
                                 ALU.is_ge), "vc")

        @blk.tensor
        def _(t):
            p.wait(t, only=["vc"])
            p.op(t.transpose(pTa[0:16, 0:128], prefs[:], cst.idn_f[:]), "tc")

        @blk.vector
        def _(v):
            p.wait(v, only=["tc"])
            p.op(v.reduce_sum(rwork[0:16, 60:61], pTa[0:16, 0:128], AX.X), "vc")

        @blk.tensor
        def _(t):
            p.wait(t, only=["vc"])
            p.op(t.transpose(pTb[0:1, 0:16], rwork[0:16, 60:61],
                             cst.idn_f[0:16, 0:16]), "tc")

        @blk.vector
        def _(v):
            p.wait(v, only=["tc"])
            p.op(v.memset(vrow[0:1, 0:32], 0.0), "vc")

        @blk.vector
        def _(v):
            p.vfence(v)
            p.op(v.tensor_copy(vrow[0:1, 1:16], pTb[0:1, 0:15]), "vc")

        @blk.vector
        def _(v):
            p.vfence(v)
            p.op(v.tensor_tensor_scan(vrow[0:1, 16:32], cst.cval[0:1, 2:3].to_broadcast([1, 16]),
                                      vrow[0:1, 0:16], cst.cval[0:1, 10:11], ALU.mult, ALU.add), "vc")

        @blk.tensor
        def _(t):
            p.wait(t, only=["vc"])
            p.op(t.matmul(pC[:, 0:16], cst.tri_f[:], prefs[:], start=True, stop=False), "tc")
            p.op(t.matmul(pC[:, 0:16], cst.ones_f[:], vrow[0:1, 16:32],
                          start=False, stop=True), "tc")

        @blk.vector
        def _(v):
            p.wait(v, only=["tc"])
            p.op(v.tensor_copy(prefs[:], pC[:, 0:16]), "vc")

        emit_cols_to_row(p, blk, cst, vv[:], 16, vrow, pTa, pTb)
        emit_row_to_rep(p, blk, cst, vrow, T, VREP, pA[0])

        @blk.vector
        def _(v):
            p.vfence(v)
            for j in range(16):
                p.op(v.tensor_scalar(dump[:, 0:T], VREP, vv[:, j:j + 1],
                                     None, ALU.is_gt, ALU.add,
                                     accum_out=cg[:, j:j + 1]), "vc")

        @blk.vector
        def _(v):
            p.vfence(v)
            p.op(v.select(dump[:, 0:16], satu[:], prefs[:], cg[:]), "vc")

        @blk.vector
        def _(v):
            p.vfence(v)
            p.op(v.tensor_scalar(sel32[:], dump[:, 0:16], 1024.0, None, ALU.is_lt), "vc")

        if S2_DBG:
            dbg_ap = S2_DBG[0]
            S2_DBG.pop(0)

            @blk.gpsimd
            def _(g):
                p.wait(g)
                p.dma(g, dbg_ap[:, 0:16], s_tm[:], sem="st")
                p.dma(g, dbg_ap[:, 16:32], vv[:], sem="st")
                p.dma(g, dbg_ap[:, 32:48], prefs[:], sem="st")
                p.dma(g, dbg_ap[:, 48:64], cg[:], sem="st")
                p.dma(g, dbg_ap[:, 64:80], sel32[:], sem="st")
                p.dma(g, dbg_ap[:, 80:96], r_tm[:], sem="st")
                p.dma(g, dbg_ap[:, 96:96 + T], VREP, sem="st")
                p.dma(g, dbg_ap[0:1, 96 + T:96 + T + T], vrow[0:1, 0:T], sem="st")

        for c in range(NCH):
            lt_tc0, lt_sc0, lt_vc0 = p.tot["tc"], p.tot["sc"], p.tot["vc"]

            @blk.tensor
            def _(t, c=c):
                p.wait(t, only=["vc", "sc", "wst", "st"])
                for ot in range(4):
                    if ot >= 2:
                        t.wait_ge(p.sems["sc"], lt_sc0 + (ot - 1))
                    for k in range(KD):
                        p.op(t.matmul(pA[ot % 2][:],
                                      lw1sb[:, k * 512 + ot * 128: k * 512 + (ot + 1) * 128],
                                      yT[:, k * T + c * LC: k * T + (c + 1) * LC],
                                      start=(k == 0), stop=(k == KD - 1)), "tc")

            @blk.vector
            def _(v, c=c):
                for ot in range(4):
                    v.wait_ge(p.sems["tc"], lt_tc0 + 8 * (ot + 1))
                    if ot >= 2:
                        v.wait_ge(p.sems["sc"], lt_sc0 + (ot - 1))
                    p.op(v.tensor_mul(u1[:, (ot % 2) * LC:(ot % 2 + 1) * LC],
                                      pA[ot % 2][:], SREP[:, c * LC:(c + 1) * LC]), "vc")

            @blk.scalar
            def _(s):
                for ot in range(4):
                    s.wait_ge(p.sems["vc"], lt_vc0 + ot + 1)
                    p.op(s.activation(hbuf[:, ot * LC:(ot + 1) * LC],
                                      u1[:, (ot % 2) * LC:(ot % 2 + 1) * LC], AF.Gelu,
                                      bias=b1l_c[:, ot:ot + 1]), "sc")

            lo_tc0, lo_vc0 = p.tot["tc"], p.tot["vc"]

            @blk.tensor
            def _(t):
                p.wait(t, only=["sc"])
                g = 0
                for lt in range(4):
                    for og in range(2):
                        if g >= 2:
                            t.wait_ge(p.sems["vc"], lo_vc0 + (g - 1))
                        for k in range(4):
                            p.op(t.matmul(pA[g % 2][:],
                                          hbuf[:, k * LC + lt * 128: k * LC + (lt + 1) * 128],
                                          lw2sb[:, k * 1024 + og * 512: k * 1024 + (og + 1) * 512],
                                          start=(k == 0), stop=False), "tc")
                        p.op(t.matmul(pA[g % 2][:], cst.ones_b[:],
                                      b2l_r[0:1, og * 512:(og + 1) * 512],
                                      start=False, stop=True), "tc")
                        g += 1

            @blk.vector
            def _(v):
                for g in range(8):
                    v.wait_ge(p.sems["tc"], lo_tc0 + 5 * (g + 1))
                    lt, og = divmod(g, 2)
                    p.op(v.tensor_copy(lightb[:, lt * D + og * 512: lt * D + (og + 1) * 512],
                                       pA[g % 2][:]), "vc")

            @blk.vector
            def _(v, c=c):
                p.vfence(v)
                for k in range(KD):
                    p.op(v.tensor_scalar(yThc[:, k * LC:(k + 1) * LC],
                                         yT[:, k * T + c * LC: k * T + (c + 1) * LC],
                                         gh_c[:, k:k + 1], None, ALU.mult), "vc")

            hh_wst0, hh_tc0, hh_sc0, hh_vc0 = (p.tot[k] for k in ("wst", "tc", "sc", "vc"))

            @blk.sync
            def _(sy):
                p.wait_prep(sy, prep, ["L0_ffn_heavy_w1"])
                sy.wait_ge(p.sems["tc"], hh_tc0)
                sy.wait_ge(p.sems["vc"], hh_vc0)
                for og in range(8):
                    if og >= 2:
                        sy.wait_ge(p.sems["tc"], hh_tc0 + 32 * (og - 1))
                    p.dma(sy, wsl1[og % 2], 
                          hw1s.rearrange("(k p) o -> p k o", p=128)[:, :, og * 512:(og + 1) * 512],
                          sem="wst")

            @blk.tensor
            def _(t, c=c):
                t.wait_ge(p.sems["vc"], hh_vc0)
                for og in range(8):
                    t.wait_ge(p.sems["wst"], hh_wst0 + 16 * (og + 1))
                    for ot in range(4):
                        g = og * 4 + ot
                        if g >= 2:
                            t.wait_ge(p.sems["vc"], hh_vc0 + (g - 1))
                        for k in range(KD):
                            p.op(t.matmul(pA[g % 2][:],
                                          wsl1[og % 2][:, k * 512 + ot * 128: k * 512 + (ot + 1) * 128],
                                          yThc[:, k * LC:(k + 1) * LC],
                                          start=(k == 0), stop=(k == KD - 1)), "tc")

            @blk.vector
            def _(v, c=c):
                for g in range(32):
                    v.wait_ge(p.sems["tc"], hh_tc0 + 8 * (g + 1))
                    if g >= 2:
                        v.wait_ge(p.sems["sc"], hh_sc0 + (g - 1))
                    p.op(v.tensor_mul(u1[:, (g % 2) * LC:(g % 2 + 1) * LC],
                                      pA[g % 2][:], SREP[:, c * LC:(c + 1) * LC]), "vc")

            @blk.scalar
            def _(s):
                for g in range(32):
                    s.wait_ge(p.sems["vc"], hh_vc0 + g + 1)
                    p.op(s.activation(hbuf[:, g * LC:(g + 1) * LC],
                                      u1[:, (g % 2) * LC:(g % 2 + 1) * LC], AF.Gelu,
                                      bias=b1h_c[:, g:g + 1]), "sc")

            ho_wst0, ho_tc0, ho_sc0, ho_vc0 = (p.tot[k] for k in ("wst", "tc", "sc", "vc"))

            @blk.sync
            def _(sy):
                p.wait_prep(sy, prep, ["L0_ffn_heavy_w2"])
                sy.wait_ge(p.sems["tc"], ho_tc0)
                for og in range(2):
                    if og >= 1:
                        sy.wait_ge(p.sems["tc"], ho_tc0 + 132 * og)
                    p.dma(sy, wslab[:],
                          hw2s.rearrange("(k p) o -> p k o", p=128)[:, :, og * 512:(og + 1) * 512],
                          sem="wst")

            def _xslice(g, og):
                return xtl2[:, (g % 2) * D + og * 512:(g % 2) * D + (og + 1) * 512]

            @blk.scalar
            def _(sq, c=c):
                p.wait(sq, only=["sc", "st"])
                for g in range(8):
                    og, lt = divmod(g, 4)
                    t0 = 4 * c + lt
                    if g >= 2:
                        og2, lt2 = divmod(g - 2, 4)
                        sq.wait_ge(p.sems["vc"], ho_vc0 + 2 * (g - 1))
                        p.dma(sq, xout_b[128 * (4 * c + lt2):128 * (4 * c + lt2 + 1),
                                         og2 * 512:(og2 + 1) * 512],
                              _xslice(g - 2, og2), sem="st")
                    p.dma(sq, xtl2[:, (g % 2) * D:((g % 2) + 1) * D],
                          xin_b[128 * t0:128 * (t0 + 1), :], sem="ld")
                for g in (6, 7):
                    og2, lt2 = divmod(g, 4)
                    sq.wait_ge(p.sems["vc"], ho_vc0 + 2 * (g + 1))
                    p.dma(sq, xout_b[128 * (4 * c + lt2):128 * (4 * c + lt2 + 1),
                                     og2 * 512:(og2 + 1) * 512],
                          _xslice(g, og2), sem="st")

            @blk.tensor
            def _(t, c=c):
                t.wait_ge(p.sems["sc"], ho_sc0)
                for og in range(2):
                    t.wait_ge(p.sems["wst"], ho_wst0 + 16 * (og + 1))
                    for lt in range(4):
                        g = og * 4 + lt
                        if g >= 2:
                            t.wait_ge(p.sems["vc"], ho_vc0 + max(0, 2 * g - 3))
                        for k in range(32):
                            p.op(t.matmul(pB[g % 2][:],
                                          hbuf[:, k * LC + lt * 128: k * LC + (lt + 1) * 128],
                                          wslab[:, k * 512:(k + 1) * 512],
                                          start=(k == 0), stop=False), "tc")
                        p.op(t.matmul(pB[g % 2][:], cst.ones_b[:],
                                      b2h_r[0:1, og * 512:(og + 1) * 512],
                                      start=False, stop=True), "tc")

            @blk.vector
            def _(v, c=c):
                ld0 = p.tot["ld"] - 16 * 8
                for og in range(2):
                    for lt in range(4):
                        g = og * 4 + lt
                        t0 = 4 * c + lt
                        v.wait_ge(p.sems["tc"], ho_tc0 + 33 * (g + 1))
                        v.wait_ge(p.sems["ld"], ld0 + 16 * (g + 1))
                        xsl = xtl2[:, (g % 2) * D + og * 512:(g % 2) * D + (og + 1) * 512]
                        p.op(v.scalar_tensor_tensor(u1[:, (g % 2) * LC:(g % 2 + 1) * LC],
                                                    pB[g % 2][:], sel32[:, t0:t0 + 1],
                                                    lightb[:, lt * D + og * 512: lt * D + (og + 1) * 512],
                                                    ALU.mult, ALU.add), "vc")
                        v.wait_ge(p.sems["vc"], p.tot["vc"])
                        p.op(v.tensor_add(xsl, xsl,
                                          u1[:, (g % 2) * LC:(g % 2 + 1) * LC]), "vc")




def emit_s6(p: Prog, cst: Consts, prep: Prep, W, xin_b, xout_b):
    nc, blk = p.nc, p.block
    LC = 512
    NCH = T // LC

    w1s = prep.scr["L1_ffn_w1_w"][1]
    w2s = prep.scr["L1_ffn_w2_w"][1]
    h1s = prep.scr["L1_ffn_h1_w"][1]
    h2s = prep.scr["L1_ffn_h2_w"][1]
    w3s = prep.scr["L1_ffn_w3_w"][1]

    with ExitStack() as st:
        sb = lambda n, sh, dt: st.enter_context(nc.sbuf_tensor(f"{n}_{p.uid()}", sh, dt))
        ps = lambda n, sh, dt=F32: st.enter_context(nc.psum_tensor(f"{n}_{p.uid()}", sh, dt))

        ycT = sb("ycT", [128, KD * LC], BF16)
        abuf = sb("abuf", [128, 32 * LC], BF16)
        bbuf = sb("bbuf", [128, 32 * LC], BF16)
        cbuf = sb("cbuf", [128, 32 * LC], BF16)
        wsl = [sb("wslA", [128, 16384], BF16), sb("wslB", [128, 16384], BF16)]
        xtl = sb("xtl", [128, D], F32)
        otT = sb("otT", [128, KD * LC], F32)
        s1c = sb("s1c", [128, 32], F32)
        s2c = sb("s2c", [128, 32], F32)
        h1c = sb("h1c", [128, 32], F32)
        h2c = sb("h2c", [128, 32], F32)
        s3c = sb("s3c", [128, 8], F32)
        cols = sb("cols", [128, 16], F32)
        tmp = sb("tmp", [128, D], F32)
        ytb = sb("ytb", [128, D], BF16)

        load_cols(p, cst, W["L1_ffn_w1_s"], 32, s1c[:, 0:32])
        load_cols(p, cst, W["L1_ffn_w2_s"], 32, s2c[:, 0:32])
        load_cols(p, cst, W["L1_ffn_h1_s"], 32, h1c[:, 0:32])
        load_cols(p, cst, W["L1_ffn_h2_s"], 32, h2c[:, 0:32])
        load_cols(p, cst, W["L1_ffn_w3_s"], 8, s3c[:, 0:8])

        pA = [ps("pA0", [128, 512]), ps("pA1", [128, 512])]
        pB = [ps("pB0", [128, 512]), ps("pB1", [128, 512])]
        pC = [ps("pC0", [128, 512]), ps("pC1", [128, 512])]
        pT = [ps("pT0", [128, 128], BF16), ps("pT1", [128, 128], BF16)]
        t1 = otT[:, 0:2 * LC]

        def slab_ap(scr, og):
            return scr[:, :, og * 512:(og + 1) * 512].transpose([1, 0, 2])

        for c in range(NCH):
            for i in range(4):
                @blk.scalar
                def _(sq, c=c, i=i):
                    p.wait(sq, only=["vc", "st", "sc"])
                    t0 = 4 * c + i
                    p.dma(sq, xtl[:], xin_b[128 * t0:128 * (t0 + 1), :], sem="ld")

                emit_ln_tile(p, blk, cst, xtl[:], cols, tmp[:], [ytb[:]])
                emit_transpose_tiles(p, blk, cst, ytb[:], ycT, i, LC, (pT[0], pT[1]))

            wst0, tc0, sc0, vc0 = (p.tot[k] for k in ("wst", "tc", "sc", "vc"))

            @blk.sync
            def _(sy, c=c):
                p.wait(sy, only=["st"])
                p.wait_prep(sy, prep, ["L1_ffn_w1_w", "L1_ffn_w2_w"])
                sy.wait_ge(p.sems["tc"], tc0)
                for og in range(8):
                    if og >= 2:
                        sy.wait_ge(p.sems["tc"], tc0 + 64 * (og - 1))
                    p.dma(sy, wsl[og % 2][:, 0:4096], slab_ap(w1s, og), sem="wst")
                    p.dma(sy, wsl[og % 2][:, 4096:8192], slab_ap(w2s, og), sem="wst")

            @blk.tensor
            def _(t):
                t.wait_ge(p.sems["vc"], vc0)
                t.wait_ge(p.sems["sc"], sc0)
                for og in range(8):
                    t.wait_ge(p.sems["wst"], wst0 + 32 * (og + 1))
                    for ot in range(4):
                        g = og * 4 + ot
                        if g >= 2:
                            t.wait_ge(p.sems["sc"], sc0 + (g - 1))
                            t.wait_ge(p.sems["vc"], vc0 + (g - 1))
                        w = wsl[og % 2]
                        for k in range(KD):
                            p.op(t.matmul(pA[g % 2][:],
                                          w[:, k * 512 + ot * 128: k * 512 + (ot + 1) * 128],
                                          ycT[:, k * LC:(k + 1) * LC],
                                          start=(k == 0), stop=(k == KD - 1)), "tc")
                        for k in range(KD):
                            p.op(t.matmul(pB[g % 2][:],
                                          w[:, 4096 + k * 512 + ot * 128: 4096 + k * 512 + (ot + 1) * 128],
                                          ycT[:, k * LC:(k + 1) * LC],
                                          start=(k == 0), stop=(k == KD - 1)), "tc")

            @blk.scalar
            def _(s):
                for g in range(32):
                    s.wait_ge(p.sems["tc"], tc0 + 16 * (g + 1))
                    if g >= 2:
                        s.wait_ge(p.sems["vc"], vc0 + (g - 1))
                    p.op(s.activation(t1[:, (g % 2) * LC:(g % 2 + 1) * LC], pA[g % 2][:],
                                      AF.Silu, scale=s1c[:, g:g + 1]), "sc")

            @blk.vector
            def _(v):
                for g in range(32):
                    v.wait_ge(p.sems["sc"], sc0 + g + 1)
                    p.op(v.scalar_tensor_tensor(abuf[:, g * LC:(g + 1) * LC],
                                                pB[g % 2][:], s2c[:, g:g + 1],
                                                t1[:, (g % 2) * LC:(g % 2 + 1) * LC],
                                                ALU.mult, ALU.mult), "vc")

            def big_stage(wsrc, src_buf, dst_buf, scale_cols, wname):
                wst0, tc0, sc0, vc0 = (p.tot[k] for k in ("wst", "tc", "sc", "vc"))

                @blk.sync
                def _(sy):
                    p.wait_prep(sy, prep, [wname])
                    sy.wait_ge(p.sems["tc"], tc0)
                    for og in range(8):
                        if og >= 2:
                            sy.wait_ge(p.sems["tc"], tc0 + 128 * (og - 1))
                        p.dma(sy, wsl[og % 2][:], slab_ap(wsrc, og), sem="wst")

                @blk.tensor
                def _(t):
                    t.wait_ge(p.sems["vc"], vc0)
                    t.wait_ge(p.sems["sc"], sc0)
                    for og in range(8):
                        t.wait_ge(p.sems["wst"], wst0 + 16 * (og + 1))
                        for ot in range(4):
                            g = og * 4 + ot
                            if g >= 2:
                                t.wait_ge(p.sems["sc"], sc0 + (g - 1))
                            for k in range(32):
                                p.op(t.matmul(pC[g % 2][:],
                                              wsl[og % 2][:, k * 512 + ot * 128: k * 512 + (ot + 1) * 128],
                                              src_buf[:, k * LC:(k + 1) * LC],
                                              start=(k == 0), stop=(k == 31)), "tc")

                @blk.scalar
                def _(s):
                    for g in range(32):
                        s.wait_ge(p.sems["tc"], tc0 + 32 * (g + 1))
                        p.op(s.activation(dst_buf[:, g * LC:(g + 1) * LC],
                                          pC[g % 2][:], AF.Gelu,
                                          scale=scale_cols[:, g:g + 1]), "sc")

            big_stage(h1s, abuf, bbuf, h1c, "L1_ffn_h1_w")
            big_stage(h2s, bbuf, cbuf, h2c, "L1_ffn_h2_w")

            wst0, tc0, sc0, vc0 = (p.tot[k] for k in ("wst", "tc", "sc", "vc"))

            @blk.sync
            def _(sy):
                p.wait_prep(sy, prep, ["L1_ffn_w3_w"])
                sy.wait_ge(p.sems["tc"], tc0)
                for og in range(2):
                    p.dma(sy, wsl[og % 2][:], slab_ap(w3s, og), sem="wst")

            @blk.tensor
            def _(t):
                t.wait_ge(p.sems["sc"], sc0)
                t.wait_ge(p.sems["vc"], vc0)
                for og in range(2):
                    t.wait_ge(p.sems["wst"], wst0 + 16 * (og + 1))
                    for ot in range(4):
                        g = og * 4 + ot
                        if g >= 2:
                            t.wait_ge(p.sems["vc"], vc0 + (g - 1))
                        for k in range(32):
                            p.op(t.matmul(pC[g % 2][:],
                                          wsl[og % 2][:, k * 512 + ot * 128: k * 512 + (ot + 1) * 128],
                                          cbuf[:, k * LC:(k + 1) * LC],
                                          start=(k == 0), stop=(k == 31)), "tc")

            @blk.vector
            def _(v):
                for g in range(8):
                    v.wait_ge(p.sems["tc"], tc0 + 32 * (g + 1))
                    p.op(v.tensor_scalar(otT[:, g * LC:(g + 1) * LC], pC[g % 2][:],
                                         s3c[:, g:g + 1], None, ALU.mult), "vc")

            for lt in range(4):
                @blk.scalar
                def _(sq, lt=lt, c=c):
                    p.wait(sq, only=["vc", "st", "sc"])
                    t0 = 4 * c + lt
                    p.dma(sq, xtl[:], xin_b[128 * t0:128 * (t0 + 1), :], sem="ld")

                @blk.tensor
                def _(t, lt=lt):
                    p.wait(t, only=["sc", "vc"])
                    for k in range(KD):
                        dst = pA[0] if k < 4 else pA[1]
                        p.op(t.transpose(dst[:, (k % 4) * 128:(k % 4 + 1) * 128],
                                         otT[:, k * LC + lt * 128: k * LC + (lt + 1) * 128],
                                         cst.idn_f[:]), "tc")

                @blk.vector
                def _(v, lt=lt):
                    p.wait(v, only=["tc", "ld"])
                    p.op(v.tensor_add(xtl[:, 0:512], xtl[:, 0:512], pA[0][:]), "vc")
                    p.op(v.tensor_add(xtl[:, 512:1024], xtl[:, 512:1024], pA[1][:]), "vc")

                @blk.scalar
                def _(sq, lt=lt, c=c):
                    p.wait(sq, only=["vc"])
                    t0 = 4 * c + lt
                    p.dma(sq, xout_b[128 * t0:128 * (t0 + 1), :], xtl[:], sem="st")



S5_STOP = [None]


def emit_s5(p: Prog, cst: Consts, prep: Prep, W, xin_b, xout_b):
    nc, blk = p.nc, p.block
    H, HD = 16, 64
    GL = T + 128

    wgT_scr = prep.scr["L1_attn_a_Wg"][1]
    wvT_scr = prep.scr["L1_attn_a_Wv"][1]
    woT_scr = prep.scr["L1_attn_a_Wo"][1]

    with ExitStack() as st:
        sb = lambda n, sh, dt: st.enter_context(nc.sbuf_tensor(f"{n}_{p.uid()}", sh, dt))
        ps = lambda n, sh, dt=F32: st.enter_context(nc.psum_tensor(f"{n}_{p.uid()}", sh, dt))

        yT = sb("yT", [128, KD * T], BF16)
        ghT = sb("ghT", [128, KD * GL], BF16)
        v_tm = sb("v_tm", [128, NTILE * D], BF16)
        outT = sb("outT", [128, KD * T], BF16)
        wslab = sb("wslab", [128, 8192], BF16)
        g_f = sb("g_f", [128, D], F32)
        ghb = sb("ghb", [128, D], BF16)
        dump4 = sb("dump4", [128, 1024], F32)
        ebuf = sb("ebuf", [128, 1024], F32)
        attnb = sb("attnb", [128, 1024], BF16)
        attnT = sb("attnT", [128, 1024], BF16)
        stg = sb("stg", [64, 2 * GL], BF16)
        nsq = sb("nsq", [128, 16], F32)
        rs16 = sb("rs16", [128, 16], F32)
        bs4 = sb("bs4", [128, 4], F32)
        iv4 = sb("iv4", [128, 4], F32)
        gfh4 = sb("gfh4", [128, 4], F32)
        ngm4 = sb("ngm4", [128, 4], F32)
        rin4 = sb("rin4", [128, 4], F32)
        inv_all = sb("inv_all", [128, 256], F32)
        sinv_c = sb("sinv_c", [128, 16], F32)
        rsvrow = sb("rsvrow", [1, 16], F32)
        rsvrep = sb("rsvrep", [128, 16], F32)
        bgrow = sb("bgrow", [1, D], BF16)
        bvrow = sb("bvrow", [1, D], BF16)
        borow = sb("borow", [1, D], BF16)
        xtl = sb("xtl", [128, D], F32)
        cols = sb("cols", [128, 16], F32)
        tmp = sb("tmp", [128, D], F32)
        ytb = sb("ytb", [128, D], BF16)

        pC4 = ps("pC4", [128, 1024])
        pO = ps("pO", [128, 512])
        pA = ps("pA", [128, 512])
        pS = ps("pS", [128, 16])
        pt0 = ps("pt0", [128, 128], BF16)
        pt1 = ps("pt1", [128, 128], BF16)

        @blk.gpsimd
        def _(g):
            p.wait(g, only=["vc", "st", "gc"])
            p.dma(g, bgrow[:], W["L1_attn_a_bg"].rearrange("(a d) -> a d", a=1))
            p.dma(g, bvrow[:], W["L1_attn_a_bv"].rearrange("(a d) -> a d", a=1))
            p.dma(g, borow[:], W["L1_attn_a_bo"].rearrange("(a d) -> a d", a=1))

        @blk.vector
        def _(v):
            p.wait(v, only=["vc"])
            for k in range(KD):
                p.op(v.memset(ghT[:, k * GL: k * GL + 128], 0.0), "vc")

        for grp in range(NTILE):
            @blk.scalar
            def _(sq, grp=grp):
                p.wait(sq, only=["vc", "st", "sc"])
                p.dma(sq, xtl[:], xin_b[128 * grp:128 * (grp + 1), :], sem="ld")

            emit_ln_tile(p, blk, cst, xtl[:], cols, tmp[:], [ytb[:]])
            emit_transpose_tiles(p, blk, cst, ytb[:], yT, grp, T, (pt0, pt1))

        if S5_STOP[0] == "ln":
            return
        @blk.sync
        def _(sy):
            p.wait(sy, only=["tc", "sc", "vc", "st"])
            p.wait_prep(sy, prep, ["L1_attn_a_Wg"])
            p.dma(sy, wslab[:], wgT_scr.transpose([1, 0, 2]), sem="wst")

        for i in range(NTILE):
            @blk.tensor
            def _(t, i=i):
                p.wait(t, only=["wst", "vc", "sc", "ld"])
                for og in range(2):
                    for k in range(KD):
                        p.op(t.matmul(pA[:] if og == 0 else pO[:],
                                      yT[:, k * T + i * 128: k * T + (i + 1) * 128],
                                      wslab[:, k * D + og * 512: k * D + (og + 1) * 512],
                                      start=(k == 0), stop=False), "tc")
                    p.op(t.matmul(pA[:] if og == 0 else pO[:], cst.ones_b[:],
                                  bgrow[0:1, og * 512:(og + 1) * 512],
                                  start=False, stop=True), "tc")

            @blk.vector
            def _(v):
                p.wait(v, only=["tc"])
                p.op(v.tensor_copy(g_f[:, 0:512], pA[:]), "vc")
                p.op(v.tensor_copy(g_f[:, 512:1024], pO[:]), "vc")

            @blk.scalar
            def _(s):
                p.wait(s, only=["vc"])
                p.op(s.activation(tmp[:], g_f[:], AF.Square), "sc")

            @blk.vector
            def _(v):
                p.wait(v, only=["sc"])
                p.op(v.reduce_sum(nsq[:].rearrange("p (h x) -> p h x", h=H),
                                  tmp[:].rearrange("p (h dd) -> p h dd", h=H), AX.X), "vc")

            @blk.scalar
            def _(s):
                p.wait(s, only=["vc"])
                p.op(s.activation(rs16[:, 0:16], nsq[:], AF.Ln, bias=cst.cval[:, 3:4]), "sc")
                p.op(s.activation(rs16[:, 0:16], rs16[:, 0:16], AF.Exp,
                                  scale=cst.cval[:, 7:8]), "sc")

            @blk.vector
            def _(v):
                p.wait(v, only=["sc"])
                p.op(v.tensor_tensor(ghb[:].rearrange("p (h dd) -> p h dd", h=H),
                                     g_f[:].rearrange("p (h dd) -> p h dd", h=H),
                                     rs16[:, 0:16].unsqueeze(2).to_broadcast([128, H, HD]),
                                     ALU.mult), "vc")

            tc0 = p.tot["tc"]
            vc0 = p.tot["vc"]

            @blk.tensor
            def _(t, i=i):
                p.wait(t, only=["vc", "sc"])
                for k in range(KD):
                    if k >= 2:
                        t.wait_ge(p.sems["vc"], vc0 + (k - 1))
                    pt = pt0 if k % 2 == 0 else pt1
                    p.op(t.transpose(pt[:, 0:128], ghb[:, 128 * k:128 * (k + 1)],
                                     cst.idn_b[:]), "tc")

            @blk.vector
            def _(v, i=i):
                for k in range(KD):
                    v.wait_ge(p.sems["tc"], tc0 + k + 1)
                    pt = pt0 if k % 2 == 0 else pt1
                    p.op(v.tensor_copy(ghT[:, k * GL + 128 + i * 128: k * GL + 128 + (i + 1) * 128],
                                       pt[:, 0:128]), "vc")

        if S5_STOP[0] == "g":
            return
        @blk.sync
        def _(sy):
            p.wait(sy, only=["tc", "sc", "vc", "st"])
            p.wait_prep(sy, prep, ["L1_attn_a_Wv"])
            p.dma(sy, wslab[:], wvT_scr.transpose([1, 0, 2]), sem="wst")

        for i in range(NTILE):
            @blk.tensor
            def _(t, i=i):
                p.wait(t, only=["wst", "vc", "sc"])
                for og in range(2):
                    for k in range(KD):
                        p.op(t.matmul(pA[:] if og == 0 else pO[:],
                                      yT[:, k * T + i * 128: k * T + (i + 1) * 128],
                                      wslab[:, k * D + og * 512: k * D + (og + 1) * 512],
                                      start=(k == 0), stop=False), "tc")
                    p.op(t.matmul(pA[:] if og == 0 else pO[:], cst.ones_b[:],
                                  bvrow[0:1, og * 512:(og + 1) * 512],
                                  start=False, stop=True), "tc")

            @blk.vector
            def _(v, i=i):
                p.wait(v, only=["tc"])
                p.op(v.tensor_copy(v_tm[:, i * D: i * D + 512], pA[:]), "vc")
                p.op(v.tensor_copy(v_tm[:, i * D + 512: (i + 1) * D], pO[:]), "vc")

        if S5_STOP[0] == "v":
            return
        def stage_odd(grp):
            @blk.vector
            def _(v, grp=grp):
                p.wait(v, only=["tc", "vc", "sc"])
                for z in range(2):
                    h = grp * 4 + 2 * z + 1
                    kb = h // 2
                    p.op(v.tensor_copy(stg[0:64, z * GL:(z + 1) * GL],
                                       ghT[64:128, kb * GL:(kb + 1) * GL]), "vc")

        def corr_mms(t, i, grp):
            for q in range(4):
                h = grp * 4 + q
                if h % 2 == 0:
                    base = ghT
                    c0 = (h // 2) * GL
                else:
                    base = stg
                    c0 = (q // 2) * GL
                p.op(t.matmul(pC4[:, q * 256:(q + 1) * 256],
                              base[0:64, c0 + 128 + i * 128: c0 + 256 + i * 128],
                              base[0:64, c0 + i * 128: c0 + 256 + i * 128],
                              start=True, stop=True), "tc")

        for grp in range(4):
            stage_odd(grp)
            for i in range(NTILE):
                mask = cst.band_t0 if i == 0 else cst.band_main

                @blk.tensor
                def _(t, i=i, grp=grp):
                    p.wait(t, only=["sc", "vc"])
                    corr_mms(t, i, grp)

                if S5_STOP[0] == "p1a":
                    continue

                @blk.vector
                def _(v, mask=mask):
                    p.wait(v, only=["tc", "vc"])
                    p.op(v.tensor_tensor(dump4[:].rearrange("p (q j) -> p q j", q=4),
                                         pC4[:].rearrange("p (q j) -> p q j", q=4),
                                         mask[:].unsqueeze(1).to_broadcast([128, 4, 256]),
                                         ALU.mult), "vc")

                if S5_STOP[0] == "p1b":
                    continue

                @blk.vector
                def _(v):
                    p.vfence(v)
                    p.op(v.reduce_sum(bs4[:].rearrange("p (q x) -> p q x", q=4),
                                      dump4[:].rearrange("p (q j) -> p q j", q=4), AX.X),
                         "vc")

                if S5_STOP[0] == "p1c":
                    continue

                @blk.vector
                def _(v):
                    p.vfence(v)
                    p.op(v.tensor_scalar(bs4[:], bs4[:], cst.cval[:, 13:14],
                                         cst.cval[:, 2:3], ALU.mult, ALU.add), "vc")

                @blk.vector
                def _(v, i=i, grp=grp):
                    p.vfence(v)
                    p.op(v.reciprocal(iv4[:], bs4[:]), "vc")

                @blk.vector
                def _(v, i=i, grp=grp):
                    p.vfence(v)
                    p.op(v.tensor_copy(inv_all[:, grp * 64 + i: grp * 64 + 64: 16], iv4[:]),
                         "vc")

        if S5_STOP[0] in ("p1", "p1a", "p1b", "p1c"):
            return
        @blk.vector
        def _(v):
            p.vfence(v)
            for h in range(H):
                p.op(v.reduce_sum(sinv_c[:, h:h + 1], inv_all[:, h * 16:(h + 1) * 16],
                                  AX.X), "vc")

        @blk.tensor
        def _(t):
            p.wait(t, only=["vc"])
            p.op(t.matmul(pS[0:1, :], cst.ones_col[:], sinv_c[:], start=True, stop=True),
                 "tc")

        @blk.vector
        def _(v):
            p.wait(v, only=["tc"])
            p.op(v.reciprocal(rsvrow[0:1, :], pS[0:1, :]), "vc")

        @blk.tensor
        def _(t):
            p.wait(t, only=["vc"])
            p.op(t.matmul(pS[:, :], cst.ones_f[:], rsvrow[0:1, :], start=True, stop=True),
                 "tc")

        @blk.vector
        def _(v):
            p.wait(v, only=["tc"])
            p.op(v.tensor_copy(rsvrep[:], pS[:, :]), "vc")

        if S5_STOP[0] == "sum":
            return
        for grp in range(4):
            stage_odd(grp)
            for i in range(NTILE):
                mask = cst.band_t0 if i == 0 else cst.band_main

                @blk.tensor
                def _(t, i=i, grp=grp):
                    p.wait(t, only=["sc", "vc"])
                    corr_mms(t, i, grp)

                @blk.vector
                def _(v, i=i, grp=grp):
                    p.wait(v, only=["tc", "vc"])
                    p.op(v.scalar_tensor_tensor(gfh4[:], inv_all[:, grp * 64 + i: grp * 64 + 64: 16],
                                                0.5, rsvrep[:, grp * 4:(grp + 1) * 4],
                                                ALU.mult, ALU.mult), "vc")

                @blk.vector
                def _(v):
                    p.vfence(v)
                    p.op(v.scalar_tensor_tensor(dump4[:].rearrange("p (q j) -> p q j", q=4),
                                                pC4[:].rearrange("p (q j) -> p q j", q=4),
                                                1.0,
                                                gfh4[:].unsqueeze(2).to_broadcast([128, 4, 256]),
                                                ALU.add, ALU.mult), "vc")

                @blk.vector
                def _(v):
                    p.vfence(v)
                    p.op(v.reduce_max(ngm4[:].rearrange("p (q x) -> p q x", q=4),
                                      dump4[:].rearrange("p (q j) -> p q j", q=4), AX.X,
                                      negate=True), "vc")

                @blk.vector
                def _(v):
                    p.vfence(v)
                    p.op(v.tensor_tensor(ebuf[:].rearrange("p (q j) -> p q j", q=4),
                                         dump4[:].rearrange("p (q j) -> p q j", q=4),
                                         ngm4[:].unsqueeze(2).to_broadcast([128, 4, 256]),
                                         ALU.add), "vc")

                @blk.scalar
                def _(s):
                    p.wait(s, only=["vc"])
                    p.op(s.activation(ebuf[:], ebuf[:], AF.Exp), "sc")

                @blk.vector
                def _(v, mask=mask):
                    p.wait(v, only=["sc"])
                    p.op(v.tensor_tensor(ebuf[:].rearrange("p (q j) -> p q j", q=4),
                                         ebuf[:].rearrange("p (q j) -> p q j", q=4),
                                         mask[:].unsqueeze(1).to_broadcast([128, 4, 256]),
                                         ALU.mult), "vc")

                @blk.vector
                def _(v):
                    p.vfence(v)
                    p.op(v.reduce_sum(rin4[:].rearrange("p (q x) -> p q x", q=4),
                                      ebuf[:].rearrange("p (q j) -> p q j", q=4), AX.X),
                         "vc")

                @blk.vector
                def _(v):
                    p.vfence(v)
                    p.op(v.reciprocal(rin4[:], rin4[:]), "vc")

                @blk.vector
                def _(v):
                    p.vfence(v)
                    p.op(v.tensor_tensor(attnb[:].rearrange("p (q j) -> p q j", q=4),
                                         ebuf[:].rearrange("p (q j) -> p q j", q=4),
                                         rin4[:].unsqueeze(2).to_broadcast([128, 4, 256]),
                                         ALU.mult), "vc")

                tc0 = p.tot["tc"]
                vc0 = p.tot["vc"]

                @blk.tensor
                def _(t):
                    p.wait(t, only=["vc", "sc"])
                    for w in range(8):
                        if w >= 2:
                            t.wait_ge(p.sems["vc"], vc0 + (w - 1))
                        pt = pt0 if w % 2 == 0 else pt1
                        p.op(t.transpose(pt[:, 0:128], attnb[:, w * 128:(w + 1) * 128],
                                         cst.idn_b[:]), "tc")

                @blk.vector
                def _(v):
                    for w in range(8):
                        v.wait_ge(p.sems["tc"], tc0 + w + 1)
                        pt = pt0 if w % 2 == 0 else pt1
                        p.op(v.tensor_copy(attnT[:, w * 128:(w + 1) * 128], pt[:, 0:128]), "vc")

                @blk.tensor
                def _(t, i=i, grp=grp):
                    p.wait(t, only=["sc", "vc"])
                    for q in range(4):
                        h = grp * 4 + q
                        jts = [i - 1, i] if i > 0 else [i]
                        for n, jt in enumerate(jts):
                            jj = jt - (i - 1) if i > 0 else 1
                            p.op(t.matmul(pO[0:64, q * 128:(q + 1) * 128],
                                          v_tm[:, jt * D + h * HD: jt * D + (h + 1) * HD],
                                          attnT[:, q * 256 + jj * 128: q * 256 + (jj + 1) * 128],
                                          start=(n == 0), stop=(n == len(jts) - 1)), "tc")

                @blk.vector
                def _(v, i=i, grp=grp):
                    p.wait(v, only=["tc"])
                    for q in range(4):
                        h = grp * 4 + q
                        kb = h // 2
                        pb = (h % 2) * 64
                        p.op(v.tensor_copy(outT[pb:pb + 64, kb * T + i * 128: kb * T + (i + 1) * 128],
                                           pO[0:64, q * 128:(q + 1) * 128]), "vc")

        if S5_STOP[0] == "p2":
            return
        @blk.sync
        def _(sy):
            p.wait(sy, only=["tc", "sc", "vc", "st"])
            p.wait_prep(sy, prep, ["L1_attn_a_Wo"])
            p.dma(sy, wslab[:], woT_scr.transpose([1, 0, 2]), sem="wst")

        for i in range(NTILE):
            @blk.scalar
            def _(sq, i=i):
                p.wait(sq, only=["vc", "st", "sc"])
                p.dma(sq, xtl[:], xin_b[128 * i:128 * (i + 1), :], sem="ld")

            @blk.tensor
            def _(t, i=i):
                p.wait(t, only=["wst", "vc", "sc", "ld"])
                for og in range(2):
                    for k in range(KD):
                        p.op(t.matmul(pA[:] if og == 0 else pO[:],
                                      outT[:, k * T + i * 128: k * T + (i + 1) * 128],
                                      wslab[:, k * D + og * 512: k * D + (og + 1) * 512],
                                      start=(k == 0), stop=False), "tc")
                    p.op(t.matmul(pA[:] if og == 0 else pO[:], cst.ones_b[:],
                                  borow[0:1, og * 512:(og + 1) * 512],
                                  start=False, stop=True), "tc")

            @blk.vector
            def _(v, i=i):
                p.wait(v, only=["tc", "ld"])
                p.op(v.tensor_add(xtl[:, 0:512], xtl[:, 0:512], pA[:]), "vc")
                p.op(v.tensor_add(xtl[:, 512:1024], xtl[:, 512:1024], pO[:]), "vc")

            @blk.scalar
            def _(sq, i=i):
                p.wait(sq, only=["vc"])
                p.dma(sq, xout_b[128 * i:128 * (i + 1), :], xtl[:], sem="st")



def emit_s1(p: Prog, cst: Consts, prep: Prep, W, xin_b, xout_b):
    nc, blk = p.nc, p.block
    ZL = 2176
    R = 16

    wbT_scr = prep.scr["L0_attn_a_Wb"][1]
    wpT_scr = prep.scr["L0_attn_a_Wp"][1]

    with ExitStack() as st:
        sb = lambda n, sh, dt: st.enter_context(nc.sbuf_tensor(f"{n}_{p.uid()}", sh, dt))
        ps = lambda n, sh, dt=F32: st.enter_context(nc.psum_tensor(f"{n}_{p.uid()}", sh, dt))

        yT = sb("yT", [128, KD * T], BF16)
        z2T = sb("z2T", [128, KD * ZL], BF16)
        ytm = sb("ytm", [128, NTILE * D], BF16)
        wbT = sb("wbT", [128, 8192], BF16)
        wpT = sb("wpT", [128, 8192], BF16)
        base = sb("base", [128, 2 * D], F32)
        vtm = sb("vtm", [128, 2 * D], BF16)
        vT = sb("vT", [128, KD * 256], BF16)
        xtl = sb("xtl", [128, 2 * D], F32)
        cols = sb("cols", [128, 16], F32)
        tmp = sb("tmp", [128, D], F32)
        ytb = sb("ytb", [128, D], BF16)
        yf = sb("yf", [128, D], F32)
        zb = sb("zb", [128, D], BF16)
        ga_c = sb("ga_c", [128, 8], F32)
        beta_b = sb("beta_b", [128, 8], BF16)
        beta_f = sb("beta_f", [128, 8], F32)
        bprow = sb("bprow", [1, D], BF16)
        iA_b = sb("iA_b", [128, 128], BF16)
        iA_f = sb("iA_f", [128, 128], F32)
        iAT_f = sb("iAT_f", [16, D], F32)
        iB_raw = sb("iB_raw", [16, D], F32)
        iBT_f = sb("iBT_f", [128, 128], F32)
        iBT_b = sb("iBT_b", [128, 128], BF16)
        accA = sb("accA", [16, D], F32)
        accB = sb("accB", [128, 128], F32)
        danew = sb("danew", [16, D], F32)
        dbnew = sb("dbnew", [128, 128], F32)
        pi_tm = sb("pi_tm", [128, 32], BF16)
        peT = sb("peT", [16, 256], BF16)
        pe_tm = sb("pe_tm", [128, 32], BF16)
        midT = sb("midT", [16, 256], BF16)
        mid_tm = sb("mid_tm", [128, 32], BF16)
        aeff = sb("aeff", [16, D], BF16)
        beff = sb("beff", [128, 128], BF16)
        u1 = sb("u1", [128, 512], F32)
        rcol = sb("rcol", [128, 4], F32)

        tmpw = tmp[0:16, :]
        tmpw2 = tmp[:, 0:128]
        pA = ps("pA", [128, 512])
        pB = ps("pB", [128, 512])
        p16 = ps("p16", [16, 512])
        pQ = ps("pQ", [128, 16])
        pc2 = ps("pc2", [128, 2])
        pt0 = ps("pt0", [128, 128], BF16)
        pt1 = ps("pt1", [128, 128], BF16)

        load_cols(p, cst, W["L0_attn_a_ln_gamma"], 8, ga_c[:, 0:8])
        load_cols(p, cst, W["L0_attn_a_ln_beta"], 8, beta_f[:, 0:8])

        @blk.vector
        def _(v):
            p.wait(v, only=["vc", "tc"])
            p.op(v.tensor_copy(beta_b[:], beta_f[:]), "vc")

        @blk.gpsimd
        def _(g):
            p.wait(g, only=["vc", "st"])
            p.wait_prep(g, prep, ["L0_attn_a_Wb", "L0_attn_a_Wp"])
            p.dma(g, wbT[:], wbT_scr.transpose([1, 0, 2]), sem="wst")
            p.dma(g, wpT[:], wpT_scr.transpose([1, 0, 2]), sem="wst")
            p.dma(g, iA_f[:], W["L0_attn_a_init_A"].rearrange("(k p) r -> p k r", p=128))
            p.dma(g, iB_raw[:], W["L0_attn_a_init_B"])

        @blk.vector
        def _(v):
            p.wait(v, only=["wst", "ld", "vc"])
            for k in range(KD):
                p.op(v.tensor_scalar(wpT[:, k * D:(k + 1) * D], wpT[:, k * D:(k + 1) * D],
                                     ga_c[:, k:k + 1], None, ALU.mult), "vc")
            p.op(v.tensor_copy(iA_b[:], iA_f[:]), "vc")

        tc0 = p.tot["tc"]
        vc0 = p.tot["vc"]

        @blk.tensor
        def _(t):
            p.wait(t, only=["ld", "vc"])
            for k in range(KD):
                if k >= 2:
                    t.wait_ge(p.sems["vc"], vc0 + (k - 1))
                pt = pA if k % 2 == 0 else pB
                p.op(t.transpose(pt[0:16, 0:128], iA_f[:, k * 16:(k + 1) * 16],
                                 cst.idn_f[:]), "tc")

        @blk.vector
        def _(v):
            for k in range(KD):
                v.wait_ge(p.sems["tc"], tc0 + k + 1)
                pt = pA if k % 2 == 0 else pB
                p.op(v.tensor_copy(iAT_f[0:16, k * 128:(k + 1) * 128], pt[0:16, 0:128]),
                     "vc")

        tc0 = p.tot["tc"]
        vc0 = p.tot["vc"]

        @blk.tensor
        def _(t):
            p.wait(t, only=["ld", "vc"])
            for k in range(KD):
                if k >= 2:
                    t.wait_ge(p.sems["vc"], vc0 + (k - 1))
                pt = pA if k % 2 == 0 else pB
                p.op(t.transpose(pt[:, 0:16], iB_raw[:, k * 128:(k + 1) * 128],
                                 cst.idn_f[0:16, 0:16]), "tc")

        @blk.vector
        def _(v):
            for k in range(KD):
                v.wait_ge(p.sems["tc"], tc0 + k + 1)
                pt = pA if k % 2 == 0 else pB
                p.op(v.tensor_copy(iBT_f[:, k * 16:(k + 1) * 16], pt[:, 0:16]), "vc")

        @blk.vector
        def _(v):
            p.vfence(v)
            p.op(v.tensor_copy(iBT_b[:], iBT_f[:]), "vc")
            p.op(v.memset(accA[:], 0.0), "vc")
            p.op(v.memset(accB[:], 0.0), "vc")
            for k in range(KD):
                p.op(v.memset(z2T[:, k * ZL + T: (k + 1) * ZL], 0.0), "vc")

        for og in range(2):
            @blk.tensor
            def _(t, og=og):
                p.wait(t, only=["vc", "sc"])
                for k in range(KD):
                    p.op(t.matmul(pA[0:1, :], beta_b[:, k:k + 1],
                                  wpT[:, k * D + og * 512: k * D + (og + 1) * 512],
                                  start=(k == 0), stop=(k == KD - 1)), "tc")

            @blk.vector
            def _(v, og=og):
                p.wait(v, only=["tc"])
                p.op(v.tensor_copy(bprow[0:1, og * 512:(og + 1) * 512], pA[0:1, :]), "vc")

        for grp in range(8):
            @blk.scalar
            def _(sq, grp=grp):
                p.wait(sq, only=["vc", "st", "sc"])
                for i in range(2):
                    t0 = 2 * grp + i
                    p.dma(sq, xtl[:, i * D:(i + 1) * D],
                          xin_b[128 * t0:128 * (t0 + 1), :], sem="ld")

            for i in range(2):
                t0 = 2 * grp + i
                x_ap = xtl[:, i * D:(i + 1) * D]
                emit_ln_tile(p, blk, cst, x_ap, cols, tmp[:], [ytb[:], yf[:]])
                emit_transpose_tiles(p, blk, cst, ytb[:], yT, t0, T, (pt0, pt1))

                @blk.vector
                def _(v, t0=t0):
                    p.wait(v, only=["sc", "vc"])
                    p.op(v.tensor_copy(ytm[:, t0 * D:(t0 + 1) * D], ytb[:]), "vc")

                emit_ln_tile(p, blk, cst, yf[:], cols, tmp[:], [zb[:]])
                emit_transpose_tiles(p, blk, cst, zb[:], z2T, t0, ZL, (pt0, pt1))

        for c in range(8):
            tiles = (2 * c, 2 * c + 1)

            @blk.scalar
            def _(sq, tiles=tiles):
                p.wait(sq, only=["vc", "st", "sc"])
                for i, t0 in enumerate(tiles):
                    p.dma(sq, xtl[:, i * D:(i + 1) * D],
                          xin_b[128 * t0:128 * (t0 + 1), :], sem="ld")

            for i, t0 in enumerate(tiles):
                for og in range(2):
                    @blk.tensor
                    def _(t, i=i, t0=t0, og=og):
                        p.wait(t, only=["vc", "sc", "wst"])
                        for k in range(KD):
                            p.op(t.matmul(pA[:],
                                          yT[:, k * T + t0 * 128: k * T + (t0 + 1) * 128],
                                          wbT[:, k * D + og * 512: k * D + (og + 1) * 512],
                                          start=(k == 0), stop=(k == KD - 1)), "tc")
                        for k in range(KD):
                            p.op(t.matmul(pB[:],
                                          z2T[:, k * ZL + t0 * 128 + 1: k * ZL + t0 * 128 + 129],
                                          wpT[:, k * D + og * 512: k * D + (og + 1) * 512],
                                          start=(k == 0), stop=False), "tc")
                        ones = cst.ones127_b if t0 == NTILE - 1 else cst.ones_b
                        p.op(t.matmul(pB[:], ones[:],
                                      bprow[0:1, og * 512:(og + 1) * 512],
                                      start=False, stop=True), "tc")

                    @blk.vector
                    def _(v, i=i, og=og):
                        p.wait(v, only=["tc"])
                        p.op(v.tensor_copy(base[:, i * D + og * 512: i * D + (og + 1) * 512],
                                           pA[:]), "vc")
                        p.op(v.tensor_copy(vtm[:, i * D + og * 512: i * D + (og + 1) * 512],
                                           pB[:]), "vc")

                tc0 = p.tot["tc"]
                vc0 = p.tot["vc"]

                @blk.tensor
                def _(t, i=i):
                    p.wait(t, only=["vc", "sc"])
                    for k in range(KD):
                        if k >= 2:
                            t.wait_ge(p.sems["vc"], vc0 + (k - 1))
                        pt = pt0 if k % 2 == 0 else pt1
                        p.op(t.transpose(pt[:, 0:128],
                                         vtm[:, i * D + k * 128: i * D + (k + 1) * 128],
                                         cst.idn_b[:]), "tc")

                @blk.vector
                def _(v, i=i):
                    for k in range(KD):
                        v.wait_ge(p.sems["tc"], tc0 + k + 1)
                        pt = pt0 if k % 2 == 0 else pt1
                        p.op(v.tensor_copy(vT[:, k * 256 + i * 128: k * 256 + (i + 1) * 128],
                                           pt[:, 0:128]), "vc")

                @blk.tensor
                def _(t, i=i, t0=t0):
                    p.wait(t, only=["vc", "sc"])
                    for k in range(KD):
                        p.op(t.matmul(pQ[:], yT[:, k * T + t0 * 128: k * T + (t0 + 1) * 128],
                                      iBT_b[:, k * 16:(k + 1) * 16],
                                      start=(k == 0), stop=(k == KD - 1)), "tc")

                @blk.vector
                def _(v, i=i):
                    p.wait(v, only=["tc"])
                    p.op(v.tensor_copy(pi_tm[:, i * 16:(i + 1) * 16], pQ[:]), "vc")

            for og in range(2):
                @blk.tensor
                def _(t, og=og):
                    p.wait(t, only=["vc", "sc"])
                    for i in range(2):
                        p.op(t.matmul(p16[:, :], pi_tm[:, i * 16:(i + 1) * 16],
                                      vtm[:, i * D + og * 512: i * D + (og + 1) * 512],
                                      start=(i == 0), stop=(i == 1)), "tc")

                @blk.vector
                def _(v, og=og):
                    p.wait(v, only=["tc"])
                    p.op(v.tensor_copy(danew[0:16, og * 512:(og + 1) * 512], p16[:, :]),
                         "vc")

            @blk.tensor
            def _(t):
                p.wait(t, only=["vc", "sc"])
                for k in range(KD):
                    p.op(t.matmul(p16[:, 0:256], iA_b[:, k * 16:(k + 1) * 16],
                                  vT[:, k * 256:(k + 1) * 256],
                                  start=(k == 0), stop=(k == KD - 1)), "tc")

            @blk.vector
            def _(v):
                p.wait(v, only=["tc"])
                p.op(v.tensor_copy(peT[0:16, 0:256], p16[:, 0:256]), "vc")

            tc0 = p.tot["tc"]

            @blk.tensor
            def _(t):
                p.wait(t, only=["vc"])
                for i in range(2):
                    pt = pt0 if i % 2 == 0 else pt1
                    p.op(t.transpose(pt[:, 0:16], peT[0:16, i * 128:(i + 1) * 128],
                                     cst.idn_b[0:16, 0:16]), "tc")

            @blk.vector
            def _(v):
                for i in range(2):
                    v.wait_ge(p.sems["tc"], tc0 + i + 1)
                    pt = pt0 if i % 2 == 0 else pt1
                    p.op(v.tensor_copy(pe_tm[:, i * 16:(i + 1) * 16], pt[:, 0:16]), "vc")

            for k in range(KD):
                @blk.tensor
                def _(t, k=k, tiles=tiles):
                    p.wait(t, only=["vc", "sc"])
                    for i, t0 in enumerate(tiles):
                        p.op(t.matmul(pQ[:],
                                      ytm[:, t0 * D + k * 128: t0 * D + (k + 1) * 128],
                                      pe_tm[:, i * 16:(i + 1) * 16],
                                      start=(i == 0), stop=(i == 1)), "tc")

                @blk.vector
                def _(v, k=k):
                    p.wait(v, only=["tc"])
                    p.op(v.tensor_copy(dbnew[:, k * 16:(k + 1) * 16], pQ[:]), "vc")

            @blk.vector
            def _(v):
                p.wait(v, only=["vc", "sc"])
                p.op(v.scalar_tensor_tensor(tmpw, accA[:], 1.0, accA[:],
                                            ALU.mult, ALU.mult,
                                            accum_out=rcol[0:16, 0:1]), "vc")
                p.op(v.scalar_tensor_tensor(tmpw2, accB[:], 1.0, accB[:],
                                            ALU.mult, ALU.mult,
                                            accum_out=rcol[:, 1:2]), "vc")

            @blk.tensor
            def _(t):
                p.wait(t, only=["vc"])
                p.op(t.matmul(pc2[0:1, 0:1], cst.ones_col[0:16, 0:1], rcol[0:16, 0:1],
                              start=True, stop=False), "tc")
                p.op(t.matmul(pc2[0:1, 0:1], cst.ones_col[:], rcol[:, 1:2],
                              start=False, stop=True), "tc")

            @blk.scalar
            def _(s):
                p.wait(s, only=["tc"])
                p.op(s.activation(rcol[0:1, 2:3], pc2[0:1, 0:1], AF.Ln,
                                  scale=cst.cval[0:1, 11:12], bias=cst.cval[0:1, 14:15]), "sc")
                p.op(s.activation(rcol[0:1, 2:3], rcol[0:1, 2:3], AF.Exp,
                                  scale=cst.cval[0:1, 15:16]), "sc")

            @blk.vector
            def _(v):
                p.wait(v, only=["sc"])
                p.op(v.tensor_scalar(rcol[0:1, 2:3], rcol[0:1, 2:3], cst.cval[0:1, 4:5],
                                     None, ALU.add), "vc")

            @blk.vector
            def _(v):
                p.vfence(v)
                p.op(v.reciprocal(rcol[0:1, 3:4], rcol[0:1, 2:3]), "vc")

            @blk.vector
            def _(v):
                p.vfence(v)
                p.op(v.tensor_scalar(rcol[0:1, 3:4], rcol[0:1, 3:4], 1.0, None, ALU.min),
                     "vc")

            @blk.tensor
            def _(t):
                p.wait(t, only=["vc"])
                p.op(t.matmul(pc2[:, 1:2], cst.ones_f[:], rcol[0:1, 3:4],
                              start=True, stop=True), "tc")

            @blk.vector
            def _(v):
                p.wait(v, only=["tc"])
                p.op(v.tensor_scalar(rcol[:, 0:1], pc2[:, 1:2], cst.cval[:, 12:13],
                                     None, ALU.mult), "vc")

            @blk.vector
            def _(v):
                p.vfence(v)
                p.op(v.scalar_tensor_tensor(aeff[0:16, :], accA[:], rcol[0:16, 0:1],
                                            iAT_f[:], ALU.mult, ALU.add), "vc")
                p.op(v.scalar_tensor_tensor(beff[:], accB[:], rcol[:, 0:1],
                                            iBT_f[:], ALU.mult, ALU.add), "vc")

            for i, t0 in enumerate(tiles):
                @blk.tensor
                def _(t, i=i, t0=t0):
                    p.wait(t, only=["vc", "sc"])
                    for k in range(KD):
                        p.op(t.matmul(pQ[:], yT[:, k * T + t0 * 128: k * T + (t0 + 1) * 128],
                                      beff[:, k * 16:(k + 1) * 16],
                                      start=(k == 0), stop=(k == KD - 1)), "tc")

                @blk.vector
                def _(v, i=i):
                    p.wait(v, only=["tc"])
                    p.op(v.tensor_copy(mid_tm[:, i * 16:(i + 1) * 16], pQ[:]), "vc")

            tc0 = p.tot["tc"]

            @blk.tensor
            def _(t):
                p.wait(t, only=["vc"])
                for i in range(2):
                    pt = pt0 if i % 2 == 0 else pt1
                    p.op(t.transpose(pt[0:16, 0:128], mid_tm[:, i * 16:(i + 1) * 16],
                                     cst.idn_b[:]), "tc")

            @blk.vector
            def _(v):
                for i in range(2):
                    v.wait_ge(p.sems["tc"], tc0 + i + 1)
                    pt = pt0 if i % 2 == 0 else pt1
                    p.op(v.tensor_copy(midT[0:16, i * 128:(i + 1) * 128], pt[0:16, 0:128]), "vc")

            for i, t0 in enumerate(tiles):
                for og in range(2):
                    @blk.tensor
                    def _(t, i=i, og=og):
                        p.wait(t, only=["vc", "sc"])
                        p.op(t.matmul(pA[:], midT[0:16, i * 128:(i + 1) * 128],
                                      aeff[0:16, og * 512:(og + 1) * 512],
                                      start=True, stop=True), "tc")

                    @blk.vector
                    def _(v, i=i, og=og):
                        p.wait(v, only=["tc", "ld"])
                        p.op(v.scalar_tensor_tensor(u1[:], pA[:], 2.0,
                                                    base[:, i * D + og * 512: i * D + (og + 1) * 512],
                                                    ALU.mult, ALU.add), "vc")

                    @blk.vector
                    def _(v, i=i, og=og):
                        p.vfence(v)
                        p.op(v.tensor_add(xtl[:, i * D + og * 512: i * D + (og + 1) * 512],
                                          xtl[:, i * D + og * 512: i * D + (og + 1) * 512],
                                          u1[:]), "vc")

                @blk.scalar
                def _(sq, i=i, t0=t0):
                    p.wait(sq, only=["vc"])
                    p.dma(sq, xout_b[128 * t0:128 * (t0 + 1), :],
                          xtl[:, i * D:(i + 1) * D], sem="st")

            @blk.vector
            def _(v):
                p.vfence(v)
                p.op(v.tensor_add(accA[:], accA[:], danew[0:16, :]), "vc")
                p.op(v.tensor_add(accB[:], accB[:], dbnew[:]), "vc")



def emit_s3(p: Prog, cst: Consts, prep: Prep, W, xin_b, xout_b):
    nc, blk = p.nc, p.block
    CL = 1024
    NCH = T // CL
    NS = 16

    bc_scr = nc.dram_tensor(f"bc_scr_{p.uid()}", [32, CL], F32, kind="Internal").ap()

    with ExitStack() as st:
        sb = lambda n, sh, dt: st.enter_context(nc.sbuf_tensor(f"{n}_{p.uid()}", sh, dt))
        ps = lambda n, sh, dt=F32: st.enter_context(nc.psum_tensor(f"{n}_{p.uid()}", sh, dt))

        yT = sb("yT", [128, KD * CL], F32)
        dTb = sb("dTb", [128, KD * CL], BF16)
        ub = sb("ub", [128, KD * CL], F32)
        yacc = sb("yacc", [128, KD * CL], F32)
        dbc = sb("dbc", [96, CL], F32)
        wdbc_raw = sb("wdbc_raw", [96, D], F32)
        WdbcT = sb("WdbcT", [128, KD * 96], F32)
        wdt_raw = sb("wdt_raw", [128, KD * 64], F32)
        WdtT = sb("WdtT", [64, D], F32)
        A_sb = sb("A_sb", [128, 128], F32)
        bdt_c = sb("bdt_c", [128, 8], F32)
        D_c = sb("D_c", [128, 8], F32)
        Brep = sb("Brep", [128, 2 * CL], F32)
        Crep = sb("Crep", [128, 2 * CL], F32)
        da_s = sb("da_s", [128, 2 * CL], F32)
        b_s = sb("b_s", [128, 2 * CL], F32)
        h_s = sb("h_s", [128, 2 * CL], F32)
        t_s = sb("t_s", [128, CL], F32)
        hstate = sb("hstate", [128, 128], F32)
        xtl = sb("xtl", [128, D], F32)
        cols = sb("cols", [128, 16], F32)
        tmp = sb("tmp", [128, D], F32)
        yf = sb("yf", [128, D], F32)

        pD = ps("pD", [128, 512])
        pT0 = ps("pT0", [128, 128])
        pT1 = ps("pT1", [128, 128])
        pX = ps("pX", [128, 512])
        pY = ps("pY", [128, 512])

        load_cols(p, cst, W["L0_attn_b_b_dt"], 8, bdt_c[:, 0:8])
        load_cols(p, cst, W["L0_attn_b_D_param"], 8, D_c[:, 0:8])

        @blk.gpsimd
        def _(g):
            p.wait(g, only=["vc", "tc", "sc"])
            p.dma(g, wdbc_raw[:], W["L0_attn_b_W_deltaBC"])
            p.dma(g, wdt_raw[:], W["L0_attn_b_W_dt"].rearrange("(k p) r -> p k r", p=128))
            p.dma(g, A_sb[:], W["L0_attn_b_A_log"].rearrange("(k p) s -> p k s", p=128))

        tc0 = p.tot["tc"]
        vc0 = p.tot["vc"]

        @blk.tensor
        def _(t):
            p.wait(t, only=["ld", "vc"])
            for k in range(KD):
                if k >= 2:
                    t.wait_ge(p.sems["vc"], vc0 + (k - 1))
                pt = pT0 if k % 2 == 0 else pT1
                p.op(t.transpose(pt[:, 0:96], wdbc_raw[:, 128 * k:128 * (k + 1)],
                                 cst.idn_f[0:96, 0:96]), "tc")

        @blk.vector
        def _(v):
            for k in range(KD):
                v.wait_ge(p.sems["tc"], tc0 + k + 1)
                pt = pT0 if k % 2 == 0 else pT1
                p.op(v.tensor_copy(WdbcT[:, k * 96:(k + 1) * 96], pt[:, 0:96]), "vc")

        tc0 = p.tot["tc"]
        vc0 = p.tot["vc"]

        @blk.tensor
        def _(t):
            p.wait(t, only=["ld", "sc", "vc"])
            for k in range(KD):
                if k >= 2:
                    t.wait_ge(p.sems["vc"], vc0 + (k - 1))
                pt = pT0 if k % 2 == 0 else pT1
                p.op(t.transpose(pt[0:64, 0:128], wdt_raw[:, 64 * k:64 * (k + 1)],
                                 cst.idn_f[:]), "tc")

        @blk.vector
        def _(v):
            for k in range(KD):
                v.wait_ge(p.sems["tc"], tc0 + k + 1)
                pt = pT0 if k % 2 == 0 else pT1
                p.op(v.tensor_copy(WdtT[0:64, 128 * k:(128 * k) + 128], pt[0:64, 0:128]), "vc")

        @blk.scalar
        def _(s):
            p.wait(s, only=["ld"])
            p.op(s.activation(tmp[:, 0:128], A_sb[:], AF.Exp), "sc")

        @blk.vector
        def _(v):
            p.wait(v, only=["sc"])
            p.op(v.tensor_scalar(A_sb[:], tmp[:, 0:128], -1.0, None, ALU.mult), "vc")
            p.op(v.memset(hstate[:], 0.0), "vc")

        for c in range(NCH):
            for ti in range(8):
                @blk.scalar
                def _(sq, c=c, ti=ti):
                    p.wait(sq, only=["vc", "st", "sc"])
                    t0 = 8 * c + ti
                    p.dma(sq, xtl[:], xin_b[128 * t0:128 * (t0 + 1), :], sem="ld")

                emit_ln_tile(p, blk, cst, xtl[:], cols, tmp[:], [yf[:]])
                emit_transpose_tiles(p, blk, cst, yf[:], yT, ti, CL,
                                     (pT0, pT1), f32=True)

            for nn in range(2):
                @blk.tensor
                def _(t, nn=nn):
                    p.wait(t, only=["sc", "vc"])
                    for k in range(KD):
                        p.op(t.matmul(pD[0:96, :],
                                      WdbcT[:, k * 96:(k + 1) * 96],
                                      yT[:, k * CL + nn * 512: k * CL + (nn + 1) * 512],
                                      start=(k == 0), stop=(k == KD - 1)), "tc")

                @blk.vector
                def _(v, nn=nn):
                    p.wait(v, only=["tc"])
                    p.op(v.tensor_copy(dbc[:, nn * 512:(nn + 1) * 512], pD[0:96, :]), "vc")

            @blk.gpsimd
            def _(g):
                p.wait(g, only=["vc"])
                p.dma(g, bc_scr, dbc[64:96, :], sem="syq")

            for dd in range(KD):
                for nn in range(2):
                    sl = dd * CL + nn * 512

                    @blk.tensor
                    def _(t, dd=dd, nn=nn):
                        p.wait(t, only=["sc", "vc"])
                        p.op(t.matmul(pD[:], WdtT[0:64, dd * 128:(dd + 1) * 128],
                                      dbc[0:64, nn * 512:(nn + 1) * 512],
                                      start=True, stop=True), "tc")

                    @blk.vector
                    def _(v, dd=dd, sl=sl):
                        p.wait(v, only=["tc"])
                        p.op(v.tensor_scalar(t_s[:, 0:512], pD[:], bdt_c[:, dd:dd + 1],
                                             None, ALU.add), "vc")

                    @blk.vector
                    def _(v):
                        p.vfence(v)
                        p.op(v.scalar_tensor_tensor(b_s[:, 0:512], t_s[:, 0:512], -1.0,
                                                    t_s[:, 0:512], ALU.mult, ALU.max), "vc")

                    @blk.scalar
                    def _(s):
                        p.wait(s, only=["vc"])
                        p.op(s.activation(h_s[:, 0:512], b_s[:, 0:512], AF.Exp,
                                          scale=cst.cval[:, 1:2]), "sc")
                        p.op(s.activation(h_s[:, 512:1024], h_s[:, 0:512], AF.Ln,
                                          bias=cst.cval[:, 2:3]), "sc")

                    @blk.vector
                    def _(v):
                        p.wait(v, only=["sc"])
                        p.op(v.tensor_scalar(da_s[:, 0:512], t_s[:, 0:512], 0.0, None,
                                             ALU.max), "vc")

                    @blk.vector
                    def _(v, sl=sl):
                        p.vfence(v)
                        p.op(v.tensor_add(t_s[:, 0:512], da_s[:, 0:512],
                                          h_s[:, 512:1024]), "vc")

                    @blk.vector
                    def _(v, sl=sl):
                        p.vfence(v)
                        p.op(v.tensor_mul(ub[:, sl:sl + 512], t_s[:, 0:512],
                                          yT[:, sl:sl + 512]), "vc")
                        p.op(v.tensor_copy(dTb[:, sl:sl + 512], t_s[:, 0:512]), "vc")

            @blk.vector
            def _(v):
                p.vfence(v)
                for dd in range(KD):
                    p.op(v.memset(yacc[:, dd * CL:(dd + 1) * CL], 0.0), "vc")

            NS_ = NS
            gc0 = p.tot["gc"]
            sc0 = p.tot["sc"]
            vc0 = p.tot["vc"]
            syq0 = p.tot["syq"]

            def unit(s_i, dd):
                return s_i * 8 + dd

            def vc_after_scan(j):
                return vc0 + 3 * j + 1

            def gc_after_b(j):
                return gc0 + 2 * j + 1

            @blk.gpsimd
            def _(g, c=c):
                p.wait(g, only=["vc", "gc", "syq"])
                for s_i in range(NS_):
                    p.dma(g, Brep[:, (s_i % 2) * CL:(s_i % 2 + 1) * CL],
                          bc_scr[s_i:s_i + 1, :].partition_broadcast(128), sem="syq")
                    p.dma(g, Crep[:, (s_i % 2) * CL:(s_i % 2 + 1) * CL],
                          bc_scr[16 + s_i:17 + s_i, :].partition_broadcast(128), sem="syq")
                    for dd in range(8):
                        j = unit(s_i, dd)
                        if dd == 0:
                            g.wait_ge(p.sems["syq"], syq0 + 32 * (s_i + 1))
                        if j >= 2:
                            g.wait_ge(p.sems["vc"], vc_after_scan(j - 2))
                        g.tensor_tensor(b_s[:, (j % 2) * CL:(j % 2 + 1) * CL],
                                        ub[:, dd * CL:(dd + 1) * CL],
                                        Brep[:, (s_i % 2) * CL:(s_i % 2 + 1) * CL],
                                        ALU.mult).then_inc(p.sems["gc"], 1)
                        p.tot["gc"] += 1
                        if j >= 2:
                            jj = j - 2
                            s2_, dd2 = divmod(jj, 8)
                            g.tensor_copy(hstate[:, jj:jj + 1],
                                          h_s[:, (jj % 2) * CL + CL - 1:(jj % 2) * CL + CL]
                                          ).then_inc(p.sems["gc"], 1)
                        else:
                            g.engine_nop().then_inc(p.sems["gc"], 1)
                        p.tot["gc"] += 1
                for jj in (126, 127):
                    g.wait_ge(p.sems["vc"], vc_after_scan(jj))
                    g.tensor_copy(hstate[:, jj:jj + 1],
                                  h_s[:, (jj % 2) * CL + CL - 1:(jj % 2) * CL + CL]
                                  ).then_inc(p.sems["gc"], 1)
                    p.tot["gc"] += 1

            @blk.scalar
            def _(s, c=c):
                p.wait(s, only=["vc"])
                for s_i in range(NS_):
                    for dd in range(8):
                        j = unit(s_i, dd)
                        if j >= 2:
                            s.wait_ge(p.sems["vc"], vc_after_scan(j - 2))
                        p.op(s.activation(da_s[:, (j % 2) * CL:(j % 2 + 1) * CL],
                                          dTb[:, dd * CL:(dd + 1) * CL], AF.Exp,
                                          scale=A_sb[:, dd * 16 + s_i: dd * 16 + s_i + 1]),
                             "sc")

            @blk.vector
            def _(v, c=c):
                for s_i in range(NS_):
                    for dd in range(8):
                        j = unit(s_i, dd)
                        v.wait_ge(p.sems["sc"], sc0 + j + 1)
                        v.wait_ge(p.sems["gc"], gc_after_b(j))
                        p.op(v.tensor_tensor_scan(h_s[:, (j % 2) * CL:(j % 2 + 1) * CL],
                                                  da_s[:, (j % 2) * CL:(j % 2 + 1) * CL],
                                                  b_s[:, (j % 2) * CL:(j % 2 + 1) * CL],
                                                  hstate[:, (s_i * 8 + dd):(s_i * 8 + dd) + 1],
                                                  ALU.mult, ALU.add), "vc")
                        p.vfence(v)
                        p.op(v.tensor_mul(t_s[:],
                                          h_s[:, (j % 2) * CL:(j % 2 + 1) * CL],
                                          Crep[:, (s_i % 2) * CL:(s_i % 2 + 1) * CL]), "vc")
                        p.vfence(v)
                        p.op(v.tensor_add(yacc[:, dd * CL:(dd + 1) * CL],
                                          yacc[:, dd * CL:(dd + 1) * CL],
                                          t_s[:]), "vc")

            @blk.vector
            def _(v):
                p.vfence(v)
                for dd in range(KD):
                    p.op(v.scalar_tensor_tensor(yacc[:, dd * CL:(dd + 1) * CL],
                                                yT[:, dd * CL:(dd + 1) * CL],
                                                D_c[:, dd:dd + 1],
                                                yacc[:, dd * CL:(dd + 1) * CL],
                                                ALU.mult, ALU.add), "vc")

            for ti in range(8):
                lcol = ti * 128

                @blk.scalar
                def _(sq, c=c, ti=ti):
                    p.wait(sq, only=["vc", "st", "sc"])
                    t0 = 8 * c + ti
                    p.dma(sq, xtl[:], xin_b[128 * t0:128 * (t0 + 1), :], sem="ld")

                @blk.tensor
                def _(t, lcol=lcol):
                    p.wait(t, only=["vc", "sc", "ld"])
                    for k in range(KD):
                        dst = pX if k < 4 else pY
                        p.op(t.transpose(dst[:, (k % 4) * 128:(k % 4 + 1) * 128],
                                         yacc[:, k * CL + lcol: k * CL + lcol + 128],
                                         cst.idn_f[:]), "tc")

                @blk.vector
                def _(v):
                    p.wait(v, only=["tc", "ld"])
                    p.op(v.tensor_add(xtl[:, 0:512], xtl[:, 0:512], pX[:]), "vc")
                    p.op(v.tensor_add(xtl[:, 512:1024], xtl[:, 512:1024], pY[:]), "vc")

                @blk.scalar
                def _(sq, c=c, ti=ti):
                    p.wait(sq, only=["vc"])
                    t0 = 8 * c + ti
                    p.dma(sq, xout_b[128 * t0:128 * (t0 + 1), :], xtl[:], sem="st")



def emit_s7(p: Prog, cst: Consts, prep: Prep, W, xin_b, xout_b):
    nc, blk = p.nc, p.block
    H, HD = 16, 64

    with ExitStack() as st:
        sb = lambda n, sh, dt: st.enter_context(nc.sbuf_tensor(f"{n}_{p.uid()}", sh, dt))
        ps = lambda n, sh, dt=F32: st.enter_context(nc.psum_tensor(f"{n}_{p.uid()}", sh, dt))

        xtl = sb("xtl", [128, D], F32)
        yf = sb("yf", [128, D], F32)
        prod = sb("prod", [128, H * H * HD], F32)
        phg = sb("phg", [128, H * H], F32)
        snn = sb("snn", [128, H * H], F32)
        nn = sb("nn", [128, H], F32)
        karg = sb("karg", [128, H * H], F32)
        outs = sb("outs", [128, D], F32)
        cols = sb("cols", [128, 16], F32)
        tmp = sb("tmp", [128, D], F32)
        sig = sb("sig", [128, 2], F32)
        s11 = sb("s11", [1, 2], F32)
        pB = ps("pB", [128, 128])

        @blk.gpsimd
        def _(g):
            p.wait(g, only=["vc", "sc", "tc"])
            p.dma(g, s11[0:1, 0:1], W["L1_attn_b_r_sigma"].rearrange("(a b) -> a b", a=1))

        @blk.tensor
        def _(t):
            p.wait(t, only=["ld", "vc"])
            p.op(t.matmul(pB[:, 0:1], cst.ones_f[:], s11[0:1, 0:1],
                          start=True, stop=True), "tc")

        @blk.vector
        def _(v):
            p.wait(v, only=["tc"])
            p.op(v.tensor_copy(sig[:, 0:1], pB[:, 0:1]), "vc")

        @blk.vector
        def _(v):
            p.vfence(v)
            p.op(v.tensor_scalar(sig[:, 1:2], sig[:, 0:1], 2.0, None, ALU.mult), "vc")

        for t0 in range(NTILE):
            @blk.scalar
            def _(sq, t0=t0):
                p.wait(sq, only=["vc", "st", "sc"])
                p.dma(sq, xtl[:], xin_b[128 * t0:128 * (t0 + 1), :], sem="ld")

            emit_ln_tile(p, blk, cst, xtl[:], cols, tmp[:], [yf[:]])

            y3 = yf[:].rearrange("p (g dd) -> p g dd", g=H)

            @blk.vector
            def _(v):
                p.vfence(v)
                p.op(v.tensor_tensor(
                    prod[:].rearrange("p (h g dd) -> p h g dd", h=H, g=H),
                    y3.unsqueeze(2).to_broadcast([128, H, H, HD]),
                    y3.unsqueeze(1).to_broadcast([128, H, H, HD]),
                    ALU.mult), "vc")

            @blk.vector
            def _(v):
                p.vfence(v)
                p.op(v.reduce_sum(phg[:].rearrange("p (h g) -> p h g", h=H),
                                  prod[:].rearrange("p (h g dd) -> p h g dd", h=H, g=H),
                                  AX.X), "vc")

            @blk.vector
            def _(v):
                p.vfence(v)
                p.op(v.tensor_scalar(nn[:], phg[:, 0:H * H:H + 1], sig[:, 0:1],
                                     None, ALU.mult), "vc")

            @blk.vector
            def _(v):
                p.vfence(v)
                p.op(v.tensor_tensor(snn[:].rearrange("p (h g) -> p h g", h=H),
                                     nn[:].unsqueeze(2).to_broadcast([128, H, H]),
                                     nn[:].unsqueeze(1).to_broadcast([128, H, H]),
                                     ALU.add), "vc")

            @blk.vector
            def _(v):
                p.vfence(v)
                p.op(v.scalar_tensor_tensor(karg[:], phg[:], sig[:, 1:2], snn[:],
                                            ALU.mult, ALU.subtract), "vc")

            @blk.scalar
            def _(s):
                p.wait(s, only=["vc"])
                p.op(s.activation(karg[:], karg[:], AF.Exp), "sc")

            @blk.vector
            def _(v):
                p.wait(v, only=["sc"])
                p.op(v.tensor_tensor(
                    prod[:].rearrange("p (h dd g) -> p h dd g", h=H, dd=HD),
                    karg[:].rearrange("p (h g) -> p h g", h=H)
                    .unsqueeze(2).to_broadcast([128, H, HD, H]),
                    yf[:].rearrange("p (g dd) -> p g dd", g=H)
                    .transpose([0, 2, 1]).unsqueeze(1).to_broadcast([128, H, HD, H]),
                    ALU.mult), "vc")

            @blk.vector
            def _(v):
                p.vfence(v)
                p.op(v.reduce_sum(outs[:].rearrange("p (h dd) -> p h dd", h=H),
                                  prod[:].rearrange("p (h dd g) -> p h dd g", h=H, dd=HD),
                                  AX.X), "vc")

            @blk.vector
            def _(v):
                p.vfence(v)
                p.op(v.tensor_add(outs[:], outs[:], yf[:]), "vc")

            @blk.vector
            def _(v):
                p.vfence(v)
                p.op(v.tensor_add(outs[:], outs[:], xtl[:]), "vc")

            @blk.scalar
            def _(sq, t0=t0):
                p.wait(sq, only=["vc"])
                p.dma(sq, xout_b[128 * t0:128 * (t0 + 1), :], outs[:], sem="st")



WEIGHT_SHAPES = {
    "L0_attn_a_Wb": [1024, 1024], "L0_attn_a_init_A": [1024, 16],
    "L0_attn_a_init_B": [16, 1024], "L0_attn_a_ln_gamma": [1024],
    "L0_attn_a_ln_beta": [1024], "L0_attn_a_Wp": [1024, 1024],
    "L0_attn_b_W_deltaBC": [96, 1024], "L0_attn_b_W_dt": [1024, 64],
    "L0_attn_b_b_dt": [1024], "L0_attn_b_A_log": [1024, 16],
    "L0_attn_b_D_param": [1024],
    "L0_ffn_routing_token": [1024], "L0_ffn_light_gamma": [1024],
    "L0_ffn_light_w1": [1024, 512], "L0_ffn_light_b1": [512],
    "L0_ffn_light_w2": [512, 1024], "L0_ffn_light_b2": [1024],
    "L0_ffn_heavy_gamma": [1024], "L0_ffn_heavy_w1": [1024, 4096],
    "L0_ffn_heavy_b1": [4096], "L0_ffn_heavy_w2": [4096, 1024],
    "L0_ffn_heavy_b2": [1024],
    "L1_attn_a_Wg": [1024, 1024], "L1_attn_a_bg": [1024],
    "L1_attn_a_Wv": [1024, 1024], "L1_attn_a_bv": [1024],
    "L1_attn_a_Wo": [1024, 1024], "L1_attn_a_bo": [1024],
    "L1_attn_b_r_sigma": [1],
    "L1_ffn_w1_w": [4096, 1024], "L1_ffn_w1_s": [4096],
    "L1_ffn_w2_w": [4096, 1024], "L1_ffn_w2_s": [4096],
    "L1_ffn_h1_w": [4096, 4096], "L1_ffn_h1_s": [4096],
    "L1_ffn_h2_w": [4096, 4096], "L1_ffn_h2_s": [4096],
    "L1_ffn_w3_w": [1024, 4096], "L1_ffn_w3_s": [1024],
}

SUB_WEIGHTS = {
    1: ["L0_attn_a_Wb", "L0_attn_a_init_A", "L0_attn_a_init_B", "L0_attn_a_ln_gamma",
        "L0_attn_a_ln_beta", "L0_attn_a_Wp"],
    2: ["L0_ffn_routing_token", "L0_ffn_light_gamma", "L0_ffn_light_w1",
        "L0_ffn_light_b1", "L0_ffn_light_w2", "L0_ffn_light_b2", "L0_ffn_heavy_gamma",
        "L0_ffn_heavy_w1", "L0_ffn_heavy_b1", "L0_ffn_heavy_w2", "L0_ffn_heavy_b2"],
    3: ["L0_attn_b_W_deltaBC", "L0_attn_b_W_dt", "L0_attn_b_b_dt", "L0_attn_b_A_log",
        "L0_attn_b_D_param"],
    5: ["L1_attn_a_Wg", "L1_attn_a_bg", "L1_attn_a_Wv", "L1_attn_a_bv",
        "L1_attn_a_Wo", "L1_attn_a_bo"],
    6: ["L1_ffn_w1_w", "L1_ffn_w1_s", "L1_ffn_w2_w", "L1_ffn_w2_s", "L1_ffn_h1_w",
        "L1_ffn_h1_s", "L1_ffn_h2_w", "L1_ffn_h2_s", "L1_ffn_w3_w", "L1_ffn_w3_s"],
    7: ["L1_attn_b_r_sigma"],
}
SUB_WEIGHTS[4] = SUB_WEIGHTS[2]
SUB_WEIGHTS[8] = SUB_WEIGHTS[6]

PREP_TRANSPOSED = {
    1: ["L0_attn_a_Wb", "L0_attn_a_Wp"],
    2: [],
    3: [],
    5: ["L1_attn_a_Wg", "L1_attn_a_Wv", "L1_attn_a_Wo"],
    6: ["L1_ffn_w1_w", "L1_ffn_w2_w", "L1_ffn_h1_w", "L1_ffn_h2_w", "L1_ffn_w3_w"],
    7: [],
}
PREP_DIRECT = {
    1: [],
    2: ["L0_ffn_light_w1", "L0_ffn_light_w2", "L0_ffn_heavy_w1", "L0_ffn_heavy_w2"],
    3: [],
    5: [],
    6: [],
    7: [],
}
PREP_TRANSPOSED[4] = PREP_TRANSPOSED[2]
PREP_DIRECT[4] = PREP_DIRECT[2]
PREP_TRANSPOSED[8] = PREP_TRANSPOSED[6]
PREP_DIRECT[8] = PREP_DIRECT[6]

def get_emitter(sub):
    if sub in (6, 8):
        return emit_s6
    if sub in (2, 4):
        return emit_s2
    if sub == 7:
        return emit_s7
    if sub == 3:
        return emit_s3
    if sub == 1:
        return emit_s1
    if sub == 5:
        return emit_s5
    return EMITTERS[sub]


EMITTERS = {}


def build_sub_program(sub):
    p = Prog()
    nc = p.nc
    xin = nc.dram_tensor("xin", [NB, T, D], F32, kind="ExternalInput").ap()
    xout = nc.dram_tensor("xout", [NB, T, D], F32, kind="ExternalOutput").ap()
    daps = declare_consts(p)
    W = {n: nc.dram_tensor(n, WEIGHT_SHAPES[n], F32, kind="ExternalInput").ap()
         for n in SUB_WEIGHTS[sub]}
    p.open()
    cst = Consts(p, daps)
    cst.load()
    prep = Prep(p, W)
    for n in PREP_TRANSPOSED[sub]:
        prep.scratch_transposed(n)
    for n in PREP_DIRECT[sub]:
        prep.scratch_direct(n)
    prep.emit()
    em = get_emitter(sub)
    for b in range(NB):
        em(p, cst, prep, W, xin[b], xout[b])

    @p.block.gpsimd
    def _(g):
        p.wait(g)

    p.close()
    return p.nc


def run_sub(sub, x_full, weights, cores=N_CORES):
    nc = build_sub_program(sub)
    consts = host_consts()
    in_maps = []
    for c in range(cores):
        m = {"xin": np.ascontiguousarray(x_full[NB * c:NB * (c + 1)])}
        m.update({k: np.ascontiguousarray(v) for k, v in consts.items()})
        for n in SUB_WEIGHTS[sub]:
            m[n] = weights[n]
        in_maps.append(m)
    res = run_bass_kernel_spmd(nc, in_maps, core_ids=list(range(cores)))
    return np.concatenate([r["xout"] for r in res.results], axis=0)



ALL_WEIGHTS = sorted(set(sum((SUB_WEIGHTS[s] for s in (1, 2, 3, 5, 6, 7)), [])))
FUSED_PREP_T = (PREP_TRANSPOSED[1] + PREP_TRANSPOSED[5] + PREP_TRANSPOSED[6])
FUSED_PREP_D = PREP_DIRECT[2]
SUB_SEQ = [1, 2, 3, 2, 5, 6, 7, 6]


def emit_barrier(p: Prog):
    blk = p.block
    snap = p.snap()

    @blk.gpsimd
    def _(g):
        p.wait(g, snap)

    @blk.vector
    def _(v):
        p.wait(v, snap)

    @blk.scalar
    def _(s):
        p.wait(s, snap)

    @blk.tensor
    def _(t):
        p.wait(t, snap)

    @blk.sync
    def _(sy):
        p.wait(sy, snap)


def build_fused_program():
    p = Prog()
    nc = p.nc
    xin = nc.dram_tensor("xin", [NB, T, D], F32, kind="ExternalInput").ap()
    xout = nc.dram_tensor("xout", [NB, T, D], F32, kind="ExternalOutput").ap()
    xa = nc.dram_tensor("xa", [NB, T, D], F32, kind="Internal").ap()
    xb = nc.dram_tensor("xb", [NB, T, D], F32, kind="Internal").ap()
    daps = declare_consts(p)
    W = {n: nc.dram_tensor(n, WEIGHT_SHAPES[n], F32, kind="ExternalInput").ap()
         for n in ALL_WEIGHTS}
    p.open()
    cst = Consts(p, daps)
    cst.load()
    prep = Prep(p, W)
    for n in FUSED_PREP_T:
        prep.scratch_transposed(n)
    for n in FUSED_PREP_D:
        prep.scratch_direct(n)
    prep.emit()
    emit_barrier(p)

    chain = [xin] + [xa if k % 2 == 0 else xb for k in range(len(SUB_SEQ) - 1)] + [xout]
    for k, sub in enumerate(SUB_SEQ):
        em = get_emitter(sub)
        for b in range(NB):
            em(p, cst, prep, W, chain[k][b], chain[k + 1][b])
            emit_barrier(p)

    @p.block.gpsimd
    def _(g):
        p.wait(g)
        g.wait_ge(p.sems["st"], p.tot["st"])

    p.close()
    return p.nc


_FUSED_CACHE = []


def kernel(**inputs):
    x = np.ascontiguousarray(inputs["L0_attn_a_x"], dtype=np.float32)
    consts = {k: np.ascontiguousarray(v) for k, v in host_consts().items()}
    if not _FUSED_CACHE:
        _FUSED_CACHE.append(build_fused_program())
    nc = _FUSED_CACHE[0]
    in_maps = []
    for c in range(N_CORES):
        m = {"xin": np.ascontiguousarray(x[NB * c:NB * (c + 1)])}
        m.update(consts)
        for n in ALL_WEIGHTS:
            m[n] = np.ascontiguousarray(inputs[n], dtype=np.float32)
        in_maps.append(m)
    res = run_bass_kernel_spmd(nc, in_maps, core_ids=list(range(N_CORES)))
    return np.concatenate([r["xout"] for r in res.results], axis=0).astype(np.float32)
